# version 11
# baseline (speedup 1.0000x reference)
from contextlib import ExitStack

import numpy as np
import ml_dtypes

import concourse.bass as bass
import concourse.mybir as mybir
from concourse.bass_utils import run_bass_kernel_spmd

F32 = mybir.dt.float32
BF16 = mybir.dt.bfloat16
I32 = mybir.dt.int32
AF = mybir.ActivationFunctionType
ALU = mybir.AluOpType
AX = mybir.AxisListType

NCORE = 8
B = 2
S = 8192
D = 2048
H, HKV, DH = 16, 4, 128
GRP = H // HKV
DFF = 8192
SL = S // NCORE
TL = B * SL
CH = 128
NCH = SL // CH
EPS = 1e-5
ASCALE = 1.0 / float(np.sqrt(DH))
NEG = -1.0e9

ENGINES = ("sync", "gpsimd", "vector", "scalar", "tensor")
BIG = 3.0e38


class Builder:
    def __init__(self, nc, es):
        self.nc = nc
        self._es = es
        self.q = {e: [] for e in ENGINES}
        self.count = {}
        self._semcache = {}
        self._scopes = []

    def sem(self, name):
        if name in self._semcache:
            return self._semcache[name]
        handle = self._es.enter_context(self.nc.semaphore(name))
        self.count[handle] = 0
        self._semcache[name] = handle
        return handle

    def sb(self, name, shape, dtype):
        es = self._scope if self._scope is not None else self._es
        return es.enter_context(self.nc.sbuf_tensor(name, shape, dtype))

    def ps(self, name, shape, dtype=F32):
        es = self._scope if self._scope is not None else self._es
        return es.enter_context(self.nc.psum_tensor(name, shape, dtype))

    def push_scope(self):
        self._scopes.append(ExitStack())

    def pop_scope(self):
        self._scopes.pop().close()

    @property
    def _scope(self):
        return self._scopes[-1] if self._scopes else None

    def barrier(self, skip=()):
        skiph = {self._semcache[n] for n in skip if n in self._semcache}
        snap = [(sem, cnt) for sem, cnt in self.count.items()
                if cnt > 0 and sem not in skiph]
        for eng in ENGINES:
            for ms in snap:
                self.wait(eng, ms)

    def emit(self, eng, fn):
        self.q[eng].append((getattr(self, "cur_stage", "x"), fn))

    SELF = ("vector", "scalar", "gpsimd")

    def op(self, eng, fname, *args, sem=None, inc=1, **kw):
        if eng in self.SELF and fname not in ("dma_start", "collective_compute"):
            esem = self.sem("E" + eng[0])
            assert sem is None or sem is esem, "use the engine sem on " + eng
            sem = esem
            prev = self.count[sem]
            if prev > 0:
                self.wait(eng, (sem, prev))
        ms = None
        if sem is not None:
            self.count[sem] += inc
            ms = (sem, self.count[sem])

        def fn(E, fname=fname, args=args, kw=kw, sem=sem, inc=inc):
            ins = getattr(E, fname)(*args, **kw)
            if sem is not None:
                ins.then_inc(sem, inc)
        self.emit(eng, fn)
        return ms

    def eng_sem(self, eng):
        return self.sem("E" + eng[0])

    def dma(self, eng, out, in_, sem, **kw):
        prev = self.count[sem]
        if prev > 0:
            self.wait(eng, (sem, prev))
        return self.op(eng, "dma_start", sem=sem, inc=16, out=out, in_=in_, **kw)

    def wait(self, eng, ms):
        if ms is None:
            return
        sem, val = ms

        def fn(E, sem=sem, val=val):
            E.wait_ge(sem, val)
        self.emit(eng, fn)

    def at(self, sem):
        return (sem, self.count[sem])

    def finalize(self):
        block = self._es.enter_context(self.nc.Block())
        nc = self.nc
        for name in ENGINES:
            ops = self.q[name]
            if not ops:
                continue

            def body(E, ops=ops):
                cur = None
                sid = None
                for stg, fn in ops:
                    if stg != cur:
                        if cur is not None:
                            nc.leave_named_scope(cur, sid, False)
                        sid, _ = nc.enter_named_scope(stg, False)
                        cur = stg
                    fn(E)
                if cur is not None:
                    nc.leave_named_scope(cur, sid, False)
            getattr(block, name)(body)


def _bf(x):
    return np.ascontiguousarray(x).astype(ml_dtypes.bfloat16)


def host_constants():
    c = {}
    c["ident_bf"] = _bf(np.eye(128, dtype=np.float32))
    c["ones1_bf"] = _bf(np.ones((128, 1), dtype=np.float32))
    RT = np.zeros((128, 128), dtype=np.float32)
    for d in range(64):
        RT[d + 64, d] = -1.0
    for d in range(64, 128):
        RT[d - 64, d] = 1.0
    c["ropeRT_bf"] = _bf(RT)
    lam = np.exp2(-8.0 * np.arange(1, H + 1, dtype=np.float64) / H)
    r = np.arange(CH, dtype=np.float64)
    dk = np.zeros((H, CH, CH), dtype=np.float64)
    for h in range(H):
        diff = r[:, None] - r[None, :]
        dk[h] = np.where(diff >= 0, np.exp(-lam[h] * diff), 0.0)
    c["ldiagT"] = np.ascontiguousarray(
        dk.transpose(0, 2, 1).astype(np.float32).reshape(H * CH, CH))
    qdec = np.exp(-lam[:, None] * (r + 1.0))
    kdec = np.exp(-lam[:, None] * (CH - 1.0 - r))
    c["qdecb"] = np.broadcast_to(
        qdec[:, None, :], (H, 128, CH)).astype(np.float32).reshape(H * 128, CH).copy()
    c["kdecc"] = np.ascontiguousarray(kdec.T.astype(np.float32))
    c["ldiagT_bf"] = _bf(c["ldiagT"])
    c["qdecb_bf"] = _bf(c["qdecb"])
    c["blkc"] = np.broadcast_to(
        np.exp(-lam * CH).astype(np.float32), (128, H)).copy()
    c["lam"] = lam
    return c


def host_per_core(inputs, consts):
    x = np.asarray(inputs["L0_attn_x"])
    lam = consts["lam"]
    inv_freq = 1.0 / 10000.0 ** (np.arange(0, DH, 2, dtype=np.float64) / DH)
    per_core = []
    for c in range(NCORE):
        m = {}
        sl = slice(c * SL, (c + 1) * SL)
        m["x0c"] = np.ascontiguousarray(np.concatenate([x[0, sl], x[1, sl]], axis=0))
        pos = np.arange(c * SL, (c + 1) * SL, dtype=np.float64)
        fr = pos[None, :] * inv_freq[:, None]
        m["ropecos"] = np.concatenate([np.cos(fr), np.cos(fr)], axis=0).astype(np.float32)
        m["ropesin"] = np.concatenate([np.sin(fr), np.sin(fr)], axis=0).astype(np.float32)
        wk_init = np.zeros((NCORE, H), dtype=np.float64)
        for cp in range(c):
            wk_init[cp] = np.exp(-lam * SL * (c - cp - 1))
        m["kvinitw"] = np.broadcast_to(
            wk_init.astype(np.float32).reshape(1, NCORE * H), (128, NCORE * H)).copy()
        i = np.arange(128)[:, None]
        kk = np.arange(1024)[None, :]
        m["l1mask"] = np.where(kk <= 8 * i + c, 0.0, NEG).astype(np.float32)
        per_core.append(m)
    return per_core


def stage_s0(b, io):
    ldc = b.sem("ldc")
    ldx = [b.sem("ldx0"), b.sem("ldx1")]
    ve = b.eng_sem("vector")
    sc = b.eng_sem("scalar")
    mm = b.sem("mm")
    stq = [b.sem("stq0"), b.sem("stq1")]
    stv = [b.sem("stv0"), b.sem("stv1")]
    sto = b.sem("sto")
    wl = [b.sem("wl0"), b.sem("wl1")]

    NT = TL // 128

    xnT = b.sb("s0xnT", [128, 16 * TL], BF16)
    ident = b.sb("s0id", [128, 128], BF16)
    ropeRT = b.sb("s0RT", [128, 128], BF16)
    cosS = b.sb("s0cos", [128, SL], F32)
    sinS = b.sb("s0sin", [128, SL], F32)
    xt = [b.sb(f"s0xt{i}", [128, D], F32) for i in range(2)]
    xnt = [b.sb(f"s0xnt{i}", [128, D], BF16) for i in range(2)]
    sq = b.sb("s0sq", [128, D], BF16)
    stat = b.sb("s0stat", [128, 8], F32)
    wbuf = [b.sb(f"s0wb{i}", [128, 16 * 128], BF16) for i in range(2)]
    wv_sb = b.sb("s0wv", [128, 16 * 512], BF16)
    qraw = b.sb("s0qraw", [128, 512], BF16)
    t1 = b.sb("s0t1", [128, 512], F32)
    t2 = b.sb("s0t2", [128, 512], F32)
    qfin = [b.sb(f"s0qf{i}", [128, 512], BF16) for i in range(2)]
    vst = [b.sb(f"s0vs{i}", [128, 512], BF16) for i in range(2)]

    epscol = b.sb("s0eps", [128, 1], F32)

    ps_tr = [b.ps(f"s0tr{i}", [128, 128], BF16) for i in range(2)]
    ps_q = b.ps("s0q", [128, 512], F32)
    ps_rot = b.ps("s0rot", [128, 512], F32)
    ps_v = [b.ps(f"s0v{i}", [128, 512], F32) for i in range(2)]

    b.dma("sync", ident[:], io["ident_bf"], ldc)
    b.dma("sync", ropeRT[:], io["ropeRT_bf"], ldc)
    b.dma("sync", cosS[:], io["ropecos"], ldc)
    b.dma("sync", sinS[:], io["ropesin"], ldc)
    ms_consts = b.at(ldc)

    b.op("vector", "memset", epscol[:], EPS, sem=ve)
    xn_done = [None] * NT
    evac_done = [None] * NT
    tr_evac = [None, None]
    prev_ms_xn = None
    for T in range(NT):
        if T >= 2:
            b.wait("sync", xn_done[T - 2])
        ms_load = b.dma("sync", xt[T % 2][:], io["x0c"][T * 128:(T + 1) * 128, :], ldx[T % 2])
        xb = xt[T % 2]
        xnb = xnt[T % 2]
        b.wait("scalar", ms_load)
        if prev_ms_xn is not None:
            b.wait("scalar", prev_ms_xn)
        b.op("scalar", "activation", sq[:], xb[:], AF.Square,
             accum_out=stat[:, 2:3], sem=sc)
        ms_ssq = b.at(sc)
        b.wait("vector", ms_load)
        b.op("vector", "reduce_sum", stat[:, 0:1], xb[:], axis=AX.X)
        b.op("vector", "tensor_scalar_mul", stat[:, 1:2], stat[:, 0:1], 1.0 / D)
        b.wait("vector", ms_ssq)
        b.op("vector", "tensor_tensor", stat[:, 5:6], stat[:, 1:2], stat[:, 1:2], ALU.mult)
        b.op("vector", "tensor_scalar", stat[:, 6:7], stat[:, 2:3], 1.0 / D, None, ALU.mult)
        b.op("vector", "tensor_sub", stat[:, 6:7], stat[:, 6:7], stat[:, 5:6], sem=ve)
        ms_var = b.at(ve)
        b.wait("scalar", ms_var)
        b.op("scalar", "activation", stat[:, 3:4], stat[:, 6:7], AF.Ln, bias=epscol[:, 0:1])
        b.op("scalar", "activation", stat[:, 4:5], stat[:, 3:4], AF.Exp,
             scale=-0.5, sem=sc)
        ms_rstd = b.at(sc)
        b.wait("vector", ms_rstd)
        b.op("vector", "tensor_tensor", stat[:, 7:8], stat[:, 1:2], stat[:, 4:5], ALU.mult)
        b.op("vector", "tensor_scalar_mul", stat[:, 7:8], stat[:, 7:8], -1.0)
        if T >= 2:
            b.wait("vector", evac_done[T - 2])
        b.op("vector", "tensor_scalar", xnb[:], xb[:], stat[:, 4:5], stat[:, 7:8],
             ALU.mult, ALU.add, sem=ve)
        ms_xn = b.at(ve)
        xn_done[T] = ms_xn
        prev_ms_xn = ms_xn
        if T == 0:
            b.wait("tensor", ms_consts)
        b.wait("tensor", ms_xn)
        for ci in range(16):
            b.wait("tensor", tr_evac[ci % 2])
            b.op("tensor", "transpose", ps_tr[ci % 2][:],
                 xnb[:, ci * 128:(ci + 1) * 128], ident[:], sem=mm)
            ms_t = b.at(mm)
            b.wait("vector", ms_t)
            b.op("vector", "tensor_copy",
                 xnT[:, ci * TL + T * 128: ci * TL + (T + 1) * 128],
                 ps_tr[ci % 2][:], sem=ve)
            tr_evac[ci % 2] = b.at(ve)
        evac_done[T] = b.at(ve)

    b.wait("sync", evac_done[-1])
    for ci in range(16):
        b.dma("sync", io["xnT"][ci * 128:(ci + 1) * 128, :],
              xnT[:, ci * TL:(ci + 1) * TL], sto)

    jobs = [(io["wqT"], h * 128, io["qT"]) for h in range(H)] + \
           [(io["wkT"], h * 128, io["kT"]) for h in range(HKV)]

    wuse = [None] * len(jobs)
    ps_q_free = None
    qraw_free = None
    rot_free = None
    qfin_free = [None, None]
    b.wait("tensor", evac_done[-1])
    b.wait("vector", ms_consts)
    for j, (wT, e0, outT) in enumerate(jobs):
        wb = wbuf[j % 2]
        if j >= 2:
            b.wait("gpsimd", wuse[j - 2])
        src = wT[:, e0:e0 + 128].rearrange("(a p) e -> p a e", p=128)
        ms_w = b.dma("gpsimd", wb[:].rearrange("p (a e) -> p a e", a=16), src, wl[j % 2])
        b.wait("tensor", ms_w)
        for tc in range(4):
            cs = slice(tc * 512, (tc + 1) * 512)
            b.wait("tensor", ps_q_free)
            for ci in range(16):
                b.op("tensor", "matmul", ps_q[:], wb[:, ci * 128:(ci + 1) * 128],
                     xnT[:, ci * TL + tc * 512: ci * TL + (tc + 1) * 512],
                     start=(ci == 0), stop=(ci == 15),
                     sem=(mm if ci == 15 else None))
            ms_mm = b.at(mm)
            wuse[j] = ms_mm
            b.wait("scalar", ms_mm)
            b.wait("scalar", qraw_free)
            b.op("scalar", "copy", qraw[:], ps_q[:], sem=sc)
            ms_qraw = b.at(sc)
            ps_q_free = ms_qraw
            b.wait("tensor", ms_qraw)
            b.wait("tensor", rot_free)
            b.op("tensor", "matmul", ps_rot[:], ropeRT[:], qraw[:],
                 start=True, stop=True, sem=mm)
            ms_rot = b.at(mm)
            pc = (tc % 2) * 512
            b.wait("vector", ms_qraw)
            b.op("vector", "tensor_tensor", t1[:], qraw[:], cosS[:, pc:pc + 512], ALU.mult)
            b.wait("vector", ms_rot)
            b.op("vector", "tensor_tensor", t2[:], ps_rot[:], sinS[:, pc:pc + 512],
                 ALU.mult, sem=ve)
            ms_rotdone = b.at(ve)
            qraw_free = ms_rotdone
            rot_free = ms_rotdone
            b.wait("vector", qfin_free[tc % 2])
            b.op("vector", "tensor_add", qfin[tc % 2][:], t1[:], t2[:], sem=ve)
            ms_fin = b.at(ve)
            b.wait("sync", ms_fin)
            ms_st = b.dma("sync", outT[e0:e0 + 128, cs], qfin[tc % 2][:], stq[tc % 2])
            qfin_free[tc % 2] = ms_st

    srcv = io["wvT"].rearrange("(a p) e -> p a e", p=128)
    ms_wv = b.dma("gpsimd", wv_sb[:].rearrange("p (a e) -> p a e", a=16), srcv, wl[0])
    b.wait("tensor", ms_wv)
    ps_v_free = [None, None]
    vst_free = [None, None]
    for tk in range(16):
        b.wait("tensor", ps_v_free[tk % 2])
        for ci in range(16):
            b.op("tensor", "matmul", ps_v[tk % 2][:],
                 xnT[:, ci * TL + tk * 128: ci * TL + (tk + 1) * 128],
                 wv_sb[:, ci * 512:(ci + 1) * 512],
                 start=(ci == 0), stop=(ci == 15), sem=(mm if ci == 15 else None))
        ms_v = b.at(mm)
        b.wait("vector", ms_v)
        b.wait("vector", vst_free[tk % 2])
        b.op("vector", "tensor_copy", vst[tk % 2][:], ps_v[tk % 2][:], sem=ve)
        ms_ev = b.at(ve)
        ps_v_free[tk % 2] = ms_ev
        b.wait("sync", ms_ev)
        ms_stv = b.dma("sync", io["vtok"][tk * 128:(tk + 1) * 128, :], vst[tk % 2][:], stv[tk % 2])
        vst_free[tk % 2] = ms_stv

    for sem in (sto, stq[0], stq[1], stv[0], stv[1]):
        b.wait("sync", b.at(sem))
    return


def stage_s1(b, io):
    ldc = b.sem("ldc")
    ldq = [b.sem("ldq0"), b.sem("ldq1")]
    ve = b.eng_sem("vector")
    sc = b.eng_sem("scalar")
    mm = b.sem("mm")
    sto1 = b.sem("sto1")

    kT_sb = b.sb("s1kT", [128, 4 * TL], BF16)
    vtok_sb = b.sb("s1v", [128, 16 * 512], BF16)
    ldiagT = b.sb("s1diag", [128, 16 * 128], BF16)
    qdecb = b.sb("s1qdec", [128, 16 * 128], BF16)
    kdecc = b.sb("s1kdec", [128, 16], F32)
    blkc = b.sb("s1blk", [128, 16], F32)
    ident = b.sb("s1id", [128, 128], BF16)
    KV = b.sb("s1KV", [128, 32 * 128], F32)
    o_sb = b.sb("s1o", [128, 16 * TL], BF16)
    qtl = [b.sb(f"s1q{i}", [128, SL], BF16) for i in range(2)]
    ktok = b.sb("s1ktok", [128, NCH * 128], BF16)
    kvbf = b.sb("s1kvbf", [128, 128], BF16)
    qsc = b.sb("s1qsc", [128, 128], BF16)
    ksc = b.sb("s1ksc", [128, 128], BF16)
    mstage = [b.sb(f"s1mst{i}", [128, 128], F32) for i in range(2)]

    ps_o = b.ps("s1po", [128, 128], F32)
    ps_s = b.ps("s1ps", [128, 128], F32)
    ps_m = b.ps("s1pm", [128, 128], F32)
    ps_tr = [b.ps(f"s1ptr{i}", [128, 128], BF16) for i in range(2)]

    b.dma("sync", kT_sb[:].rearrange("p (a c) -> p a c", a=4),
          io["kT"].rearrange("(a p) c -> p a c", p=128), ldc)
    b.dma("sync", vtok_sb[:].rearrange("p (a c) -> p a c", a=16),
          io["vtok"].rearrange("(a p) c -> p a c", p=128), ldc)
    b.dma("sync", ldiagT[:].rearrange("p (a c) -> p a c", a=16),
          io["ldiagT_bf"].rearrange("(a p) c -> p a c", p=128), ldc)
    b.dma("sync", qdecb[:].rearrange("p (a c) -> p a c", a=16),
          io["qdecb_bf"].rearrange("(a p) c -> p a c", p=128), ldc)
    b.dma("sync", kdecc[:], io["kdecc"], ldc)
    b.dma("sync", blkc[:], io["blkc"], ldc)
    b.dma("sync", ident[:], io["ident_bf"], ldc)
    ms_c = b.at(ldc)
    ms_kv0 = b.op("vector", "memset", KV[:], 0.0)
    b.wait("tensor", ms_c)

    tr_free = [None, None]
    q_use = [None, None]
    mst_st = [None, None]
    hidx = 0
    for bb in range(B):
        for kvh in range(HKV):
            for t in range(NCH):
                b.wait("tensor", tr_free[t % 2])
                ms_tr = b.op("tensor", "transpose", ps_tr[t % 2][:],
                             kT_sb[:, kvh * TL + bb * SL + t * 128:
                                   kvh * TL + bb * SL + (t + 1) * 128],
                             ident[:], sem=mm)
                b.wait("vector", ms_tr)
                ms_ev = b.op("vector", "tensor_copy",
                             ktok[:, t * 128:(t + 1) * 128], ps_tr[t % 2][:])
                tr_free[t % 2] = ms_ev
            ms_ktok = tr_free[(NCH - 1) % 2]
            b.wait("tensor", ms_ktok)
            for hq in range(GRP):
                h = kvh * GRP + hq
                bh = bb * H + h
                slot = hidx % 2
                b.wait("sync", q_use[slot])
                ms_q = b.dma("sync", qtl[slot][:],
                             io["qT"][h * 128:(h + 1) * 128, bb * SL:(bb + 1) * SL],
                             ldq[slot])
                b.wait("vector", ms_q)
                b.wait("tensor", ms_q)
                for t in range(NCH):
                    qc = qtl[slot][:, t * 128:(t + 1) * 128]
                    kc = kT_sb[:, kvh * TL + bb * SL + t * 128:
                               kvh * TL + bb * SL + (t + 1) * 128]
                    vc = vtok_sb[:, (bb * NCH + t) * 512 + kvh * 128:
                                 (bb * NCH + t) * 512 + (kvh + 1) * 128]
                    ktc = ktok[:, t * 128:(t + 1) * 128]
                    ms_kvbf = b.op("vector", "tensor_copy", kvbf[:],
                                   KV[:, bh * 128:(bh + 1) * 128])
                    ms_qsc = b.op("vector", "tensor_tensor", qsc[:], qc,
                                  qdecb[:, h * 128:(h + 1) * 128], ALU.mult)
                    ms_ksc = b.op("vector", "tensor_scalar", ksc[:], ktc,
                                  kdecc[:, h:h + 1], None, ALU.mult)
                    b.wait("tensor", ms_qsc)
                    b.op("tensor", "matmul", ps_o[:], kvbf[:], qsc[:],
                         start=True, stop=False)
                    ms_sT = b.op("tensor", "matmul", ps_s[:], kc, qc,
                                 start=True, stop=True, sem=mm)
                    b.wait("vector", ms_sT)
                    ms_stl = b.op("vector", "tensor_tensor", qsc[:],
                                  ps_s[:], ldiagT[:, h * 128:(h + 1) * 128], ALU.mult)
                    b.wait("tensor", ms_stl)
                    ms_o = b.op("tensor", "matmul", ps_o[:], vc, qsc[:],
                                start=False, stop=True, sem=mm)
                    b.wait("tensor", ms_ksc)
                    ms_m = b.op("tensor", "matmul", ps_m[:], ksc[:], vc,
                                start=True, stop=True, sem=mm)
                    b.wait("vector", ms_o)
                    ms_oe = b.op("vector", "tensor_copy",
                                 o_sb[:, h * TL + bb * SL + t * 128:
                                      h * TL + bb * SL + (t + 1) * 128], ps_o[:])
                    b.wait("vector", ms_m)
                    ms_upd = b.op("vector", "scalar_tensor_tensor",
                                  KV[:, bh * 128:(bh + 1) * 128],
                                  KV[:, bh * 128:(bh + 1) * 128],
                                  blkc[:, h:h + 1], ps_m[:], ALU.mult, ALU.add)
                q_use[slot] = ms_m
                b.wait("vector", mst_st[hidx % 2])
                ms_ms = b.op("vector", "tensor_copy", mstage[hidx % 2][:],
                             KV[:, bh * 128:(bh + 1) * 128])
                b.wait("sync", ms_ms)
                mst_st[hidx % 2] = b.dma(
                    "sync", io["M_core"][bh * 128:(bh + 1) * 128, :],
                    mstage[hidx % 2][:], sto1)
                hidx += 1

    b.wait("sync", b.at(ve))
    for h in range(H):
        b.dma("sync", io["o_loc"][h * 128:(h + 1) * 128, :],
              o_sb[:, h * TL:(h + 1) * TL], sto1)
    b.wait("sync", b.at(sto1))
    return


def stage_s2(b, io):
    ldc = b.sem("ldc")
    ve = b.eng_sem("vector")
    sc = b.eng_sem("scalar")
    mm = b.sem("mm")
    wlg = [b.sem("wlg0"), b.sem("wlg1")]
    wlu = [b.sem("wlu0"), b.sem("wlu1")]
    wld = [b.sem("wld0"), b.sem("wld1")]
    ldh = [b.sem("ldh0"), b.sem("ldh1")]
    sth = [b.sem("sth0"), b.sem("sth1")]
    stg = [b.sem("stg0"), b.sem("stg1")]

    b.push_scope()
    xnT = b.sb("s2xnT", [128, 16 * TL], BF16)
    c128 = b.sb("s2c128", [128, 128], BF16)
    c1 = b.sb("s2c1", [128, TL], F32)
    gsc = b.sb("s2gsc", [128, 64], F32)
    usc = b.sb("s2usc", [128, 64], F32)
    wgb = [b.sb(f"s2wg{i}", [128, 16 * 512], BF16) for i in range(2)]
    wub = [b.sb(f"s2wu{i}", [128, 16 * 512], BF16) for i in range(2)]
    tg = b.sb("s2tg", [128, 2 * 512], F32)
    tu = b.sb("s2tu", [128, 2 * 512], F32)
    sg = b.sb("s2sg", [128, 2 * 512], BF16)
    hline = [b.sb(f"s2hl{i}", [128, TL], BF16) for i in range(2)]
    ps_c = b.ps("s2pc", [128, 512], F32)
    ps_g = [b.ps(f"s2pg{i}", [128, 512], F32) for i in range(2)]
    ps_u = [b.ps(f"s2pu{i}", [128, 512], F32) for i in range(2)]

    ms_xn = b.dma("sync", xnT[:].rearrange("p (a c) -> p a c", a=16),
                  io["xnT"].rearrange("(a p) c -> p a c", p=128), ldc)
    b.dma("sync", gsc[:], io["gsc_p"], ldc)
    ms_usc = b.dma("sync", usc[:], io["usc_p"], ldc)
    ms_c128 = b.op("vector", "memset", c128[:], -128.0)
    b.wait("tensor", ms_xn)
    b.wait("tensor", ms_c128)
    ps_c_free = None
    for tcol in range(4):
        b.wait("tensor", ps_c_free)
        for ci in range(16):
            b.op("tensor", "matmul", ps_c[:], c128[:],
                 xnT[:, ci * TL + tcol * 512: ci * TL + (tcol + 1) * 512],
                 start=(ci == 0), stop=(ci == 15), sem=(mm if ci == 15 else None))
        ms = b.at(mm)
        b.wait("vector", ms)
        ps_c_free = b.op("vector", "tensor_copy",
                         c1[:, tcol * 512:(tcol + 1) * 512], ps_c[:])
    ms_c1 = b.at(ve)
    b.wait("scalar", ms_usc)
    b.wait("scalar", ms_c1)

    wg_use = [None, None]
    wu_use = [None, None]
    hl_store = [None, None]
    ps_g_free = [None, None]
    ps_u_free = [None, None]
    for sl in range(16):
        b.wait("gpsimd", wg_use[sl % 2])
        ms_wg = b.dma("gpsimd", wgb[sl % 2][:].rearrange("p (a c) -> p a c", a=16),
                      io["gate_wT"][:, sl * 512:(sl + 1) * 512]
                      .rearrange("(a p) c -> p a c", p=128), wlg[sl % 2])
        b.wait("gpsimd", wu_use[sl % 2])
        ms_wu = b.dma("gpsimd", wub[sl % 2][:].rearrange("p (a c) -> p a c", a=16),
                      io["up_wT"][:, sl * 512:(sl + 1) * 512]
                      .rearrange("(a p) c -> p a c", p=128), wlu[sl % 2])
        b.wait("tensor", ms_wg)
        b.wait("tensor", ms_wu)
        for isub in range(4):
            ic = sl * 4 + isub
            hl = hline[ic % 2]
            for tcol in range(4):
                pslot = tcol % 2
                b.wait("tensor", ps_g_free[pslot])
                for ci in range(16):
                    b.op("tensor", "matmul", ps_g[pslot][:],
                         wgb[sl % 2][:, ci * 512 + isub * 128: ci * 512 + (isub + 1) * 128],
                         xnT[:, ci * TL + tcol * 512: ci * TL + (tcol + 1) * 512],
                         start=(ci == 0), stop=(ci == 15), sem=(mm if ci == 15 else None))
                ms_g = b.at(mm)
                b.wait("tensor", ps_u_free[pslot])
                for ci in range(16):
                    b.op("tensor", "matmul", ps_u[pslot][:],
                         wub[sl % 2][:, ci * 512 + isub * 128: ci * 512 + (isub + 1) * 128],
                         xnT[:, ci * TL + tcol * 512: ci * TL + (tcol + 1) * 512],
                         start=(ci == 0), stop=(ci == 15), sem=(mm if ci == 15 else None))
                ms_u = b.at(mm)
                b.wait("vector", ms_g)
                tgs = tg[:, pslot * 512:(pslot + 1) * 512]
                tus = tu[:, pslot * 512:(pslot + 1) * 512]
                sgs = sg[:, pslot * 512:(pslot + 1) * 512]
                ms_tg = b.op("vector", "tensor_tensor", tgs, ps_g[pslot][:],
                             c1[:, tcol * 512:(tcol + 1) * 512], ALU.add)
                ps_g_free[pslot] = ms_tg
                b.wait("scalar", ms_tg)
                ms_sg = b.op("scalar", "activation", sgs, tgs, AF.Silu,
                             scale=gsc[:, ic:ic + 1])
                b.wait("vector", ms_u)
                ms_tu = b.op("vector", "tensor_tensor", tus, ps_u[pslot][:],
                             c1[:, tcol * 512:(tcol + 1) * 512], ALU.add)
                ps_u_free[pslot] = ms_tu
                b.wait("vector", ms_sg)
                if tcol == 0:
                    b.wait("vector", hl_store[ic % 2])
                ms_h = b.op("vector", "scalar_tensor_tensor",
                            hl[:, tcol * 512:(tcol + 1) * 512], tus,
                            usc[:, ic:ic + 1], sgs, ALU.mult, ALU.mult)
            b.wait("sync", ms_h)
            hl_store[ic % 2] = b.dma(
                "sync", io["h_hbm"][ic * 128:(ic + 1) * 128, :], hl[:], sth[ic % 2])
        wg_use[sl % 2] = b.at(mm)
        wu_use[sl % 2] = b.at(mm)
    b.wait("sync", b.at(sth[0]))
    b.wait("sync", b.at(sth[1]))
    b.barrier()
    b.pop_scope()

    b.push_scope()
    y_acc = b.sb("s2y", [128, 16 * 1024], F32)
    rs_sb = b.sb("s2rs", [128, 16], F32)
    rsn = b.sb("s2rsn", [128, 16], F32)
    dsc = b.sb("s2dsc", [128, D], F32)
    ones1 = b.sb("s2one", [128, 1], BF16)
    wdb = [b.sb(f"s2wd{i}", [128, 8 * 1024], BF16) for i in range(2)]
    hb = [b.sb(f"s2hb{i}", [128, 8 * TL], BF16) for i in range(2)]
    ty = b.sb("s2ty", [128, 1024], F32)
    g0f = [b.sb(f"s2g0{i}", [128, 1024], F32) for i in range(2)]
    ps_y = [b.ps(f"s2py{i}", [128, 1024], F32) for i in range(2)]
    ps_rs = b.ps("s2prs", [128, 1], F32)

    b.dma("sync", dsc[:], io["dsc_b"], ldc)
    ms_ones = b.dma("sync", ones1[:], io["ones1_bf"], ldc)
    b.wait("tensor", ms_ones)

    wd_use = [None, None]
    h_use = [None, None]
    g0_store = [None, None]
    ps_y_free = [None, None]
    ps_rs_free = None
    for blk in range(2):
        for mac in range(8):
            b.wait("gpsimd", wd_use[mac % 2])
            ms_wd = b.dma("gpsimd", wdb[mac % 2][:].rearrange("p (a c) -> p a c", a=8),
                          io["down_wT"][mac * 1024:(mac + 1) * 1024,
                                        blk * 1024:(blk + 1) * 1024]
                          .rearrange("(a p) c -> p a c", p=128), wld[mac % 2])
            b.wait("sync", h_use[mac % 2])
            ms_h = b.dma("sync", hb[mac % 2][:].rearrange("p (a c) -> p a c", a=8),
                         io["h_hbm"][mac * 1024:(mac + 1) * 1024, :]
                         .rearrange("(a p) c -> p a c", p=128), ldh[mac % 2])
            b.wait("tensor", ms_wd)
            b.wait("tensor", ms_h)
            for tk in range(16):
                yslot = tk % 2
                b.wait("tensor", ps_y_free[yslot])
                for hc in range(2):
                    for ic in range(8):
                        b.op("tensor", "matmul", ps_y[yslot][:, hc * 512:(hc + 1) * 512],
                             hb[mac % 2][:, ic * TL + tk * 128: ic * TL + (tk + 1) * 128],
                             wdb[mac % 2][:, ic * 1024 + hc * 512: ic * 1024 + (hc + 1) * 512],
                             start=(ic == 0), stop=(ic == 7),
                             sem=(mm if (ic == 7 and hc == 1) else None))
                ms_y = b.at(mm)
                if blk == 0:
                    b.wait("tensor", ps_rs_free)
                    for ic in range(8):
                        b.op("tensor", "matmul", ps_rs[:],
                             hb[mac % 2][:, ic * TL + tk * 128: ic * TL + (tk + 1) * 128],
                             ones1[:], start=(ic == 0), stop=(ic == 7),
                             sem=(mm if ic == 7 else None))
                    ms_rs = b.at(mm)
                b.wait("vector", ms_y)
                if mac == 0:
                    ms_ya = b.op("vector", "tensor_copy",
                                 y_acc[:, tk * 1024:(tk + 1) * 1024], ps_y[yslot][:])
                else:
                    ms_ya = b.op("vector", "tensor_tensor",
                                 y_acc[:, tk * 1024:(tk + 1) * 1024],
                                 y_acc[:, tk * 1024:(tk + 1) * 1024], ps_y[yslot][:], ALU.add)
                ps_y_free[yslot] = ms_ya
                if blk == 0:
                    b.wait("vector", ms_rs)
                    if mac == 0:
                        ms_rse = b.op("vector", "tensor_copy", rs_sb[:, tk:tk + 1], ps_rs[:])
                    else:
                        ms_rse = b.op("vector", "tensor_tensor", rs_sb[:, tk:tk + 1],
                                      rs_sb[:, tk:tk + 1], ps_rs[:], ALU.add)
                    ps_rs_free = ms_rse
            wd_use[mac % 2] = b.at(mm)
            h_use[mac % 2] = b.at(mm)
        if blk == 0:
            ms_rsn = b.op("vector", "tensor_scalar_mul", rsn[:], rs_sb[:], -128.0)
        for tk in range(16):
            ms_t = b.op("vector", "tensor_scalar", ty[:],
                        y_acc[:, tk * 1024:(tk + 1) * 1024], rsn[:, tk:tk + 1],
                        None, ALU.add)
            b.wait("vector", g0_store[tk % 2])
            ms_y2 = b.op("vector", "tensor_tensor", g0f[tk % 2][:], ty[:],
                         dsc[:, blk * 1024:(blk + 1) * 1024], ALU.mult)
            b.wait("scalar", ms_y2)
            ms_g0 = b.op("scalar", "activation", g0f[tk % 2][:], g0f[tk % 2][:], AF.Silu)
            b.wait("sync", ms_g0)
            g0_store[tk % 2] = b.dma(
                "sync", io["g0"][tk * 128:(tk + 1) * 128, blk * 1024:(blk + 1) * 1024],
                g0f[tk % 2][:], stg[tk % 2])
    b.wait("sync", b.at(stg[0]))
    b.wait("sync", b.at(stg[1]))
    b.pop_scope()
    return


def stage_s3(b, io):
    if "ccM" in b._semcache:
        b.wait("sync", (b.sem("ccM"), 1))
    ldc = b.sem("ldc")
    ldq = [b.sem("ldq0"), b.sem("ldq1")]
    ldm = [b.sem("ldm0"), b.sem("ldm1")]
    ldx = [b.sem("ldx0"), b.sem("ldx1")]
    ldg = [b.sem("ldg0"), b.sem("ldg1")]
    ve = b.eng_sem("vector")
    sc = b.eng_sem("scalar")
    mm = b.sem("mm")
    stx = [b.sem("stx0"), b.sem("stx1")]

    o_sb = b.sb("s3o", [128, 16 * TL], BF16)
    wo_sb = b.sb("s3wo", [128, 16 * D], BF16)
    kviw = b.sb("s3kviw", [128, NCORE * H], F32)
    qdecb = b.sb("s3qdec", [128, 16 * 128], BF16)
    blkc = b.sb("s3blk", [128, 16], F32)
    Mb = [b.sb(f"s3M{i}", [128, NCORE * 128], F32) for i in range(2)]
    qtl = [b.sb(f"s3q{i}", [128, SL], BF16) for i in range(2)]
    kvcur = b.sb("s3kvc", [128, 128], F32)
    kvbf = b.sb("s3kvb", [128, 128], BF16)
    qsc = b.sb("s3qsc", [128, 128], BF16)
    xt = [b.sb(f"s3x{i}", [128, D], F32) for i in range(2)]
    gt = [b.sb(f"s3g{i}", [128, D], F32) for i in range(2)]
    tmul = b.sb("s3tm", [128, 512], F32)
    x1t = [b.sb(f"s3x1{i}", [128, D], F32) for i in range(2)]

    ps_corr = b.ps("s3pc", [128, 128], F32)
    ps_s0 = [b.ps(f"s3ps{i}", [128, 512], F32) for i in range(2)]

    ms_o = b.dma("sync", o_sb[:].rearrange("p (a c) -> p a c", a=16),
                 io["o_loc"].rearrange("(a p) c -> p a c", p=128), ldc)
    ms_wo = b.dma("gpsimd", wo_sb[:].rearrange("p (a c) -> p a c", a=16),
                  io["woT"].rearrange("(a p) c -> p a c", p=128), ldc if False else b.sem("wlo"))
    b.dma("sync", kviw[:], io["kvinitw"], ldc)
    b.dma("sync", qdecb[:].rearrange("p (a c) -> p a c", a=16),
          io["qdecb_bf"].rearrange("(a p) c -> p a c", p=128), ldc)
    ms_blk = b.dma("sync", blkc[:], io["blkc"], ldc)
    ms_consts = b.at(ldc)

    b.wait("vector", ms_consts)
    M_use = [None, None]
    q_use = [None, None]
    ps_corr_free = None
    for bb in range(B):
        for h in range(H):
            bh = bb * H + h
            b.wait("sync", M_use[bh % 2])
            src = io["M_all"].rearrange("(r n) c -> r n c", r=NCORE)[
                :, bh * 128:(bh + 1) * 128, :].rearrange("r n c -> n r c")
            ms_M = b.dma("sync", Mb[bh % 2][:].rearrange("p (r c) -> p r c", r=NCORE),
                         src, ldm[bh % 2])
            b.wait("sync", q_use[bh % 2])
            ms_q = b.dma("sync", qtl[bh % 2][:],
                         io["qT"][h * 128:(h + 1) * 128, bb * SL:(bb + 1) * SL],
                         ldq[bh % 2])
            b.wait("vector", ms_M)
            ms = b.op("vector", "tensor_scalar", kvcur[:],
                      Mb[bh % 2][:, 0:128], kviw[:, 0 * H + h:0 * H + h + 1], None,
                      ALU.mult)
            for cp in range(1, NCORE):
                ms = b.op("vector", "scalar_tensor_tensor", kvcur[:],
                          Mb[bh % 2][:, cp * 128:(cp + 1) * 128],
                          kviw[:, cp * H + h:cp * H + h + 1],
                          kvcur[:], ALU.mult, ALU.add)
            M_use[bh % 2] = ms
            b.wait("vector", ms_q)
            for t in range(NCH):
                ms_kvbf = b.op("vector", "tensor_copy", kvbf[:], kvcur[:])
                ms_qsc = b.op("vector", "tensor_tensor", qsc[:],
                              qtl[bh % 2][:, t * 128:(t + 1) * 128],
                              qdecb[:, h * 128:(h + 1) * 128], ALU.mult)
                b.wait("tensor", ms_qsc)
                b.wait("tensor", ps_corr_free)
                ms_cmm = b.op("tensor", "matmul", ps_corr[:], kvbf[:], qsc[:],
                              start=True, stop=True, sem=mm)
                b.wait("vector", ms_cmm)
                oc = o_sb[:, h * TL + bb * SL + t * 128: h * TL + bb * SL + (t + 1) * 128]
                ms_oadd = b.op("vector", "tensor_tensor", oc, oc, ps_corr[:], ALU.add)
                ps_corr_free = ms_oadd
                if t < NCH - 1:
                    b.op("vector", "tensor_scalar", kvcur[:], kvcur[:],
                         blkc[:, h:h + 1], None, ALU.mult)
            q_use[bh % 2] = ms_cmm

    b.wait("tensor", b.at(ve))
    b.wait("tensor", ms_wo)
    x_use = [None, None]
    g_use = [None, None]
    x1_st = [None, None]
    ps_free = [None, None]
    for tk in range(16):
        b.wait("sync", x_use[tk % 2])
        ms_x = b.dma("sync", xt[tk % 2][:], io["x0c"][tk * 128:(tk + 1) * 128, :],
                     ldx[tk % 2])
        b.wait("sync", g_use[tk % 2])
        ms_g = b.dma("sync", gt[tk % 2][:], io["g0"][tk * 128:(tk + 1) * 128, :],
                     ldg[tk % 2])
        for dcol in range(4):
            b.wait("tensor", ps_free[dcol % 2])
            for ci in range(16):
                b.op("tensor", "matmul", ps_s0[dcol % 2][:],
                     o_sb[:, ci * TL + tk * 128: ci * TL + (tk + 1) * 128],
                     wo_sb[:, ci * D + dcol * 512: ci * D + (dcol + 1) * 512],
                     start=(ci == 0), stop=(ci == 15), sem=(mm if ci == 15 else None))
            ms_s0 = b.at(mm)
            b.wait("vector", ms_s0)
            if dcol == 0:
                b.wait("vector", ms_x)
                b.wait("vector", ms_g)
                b.wait("vector", x1_st[tk % 2])
            ms_t = b.op("vector", "tensor_tensor", tmul[:], ps_s0[dcol % 2][:],
                        gt[tk % 2][:, dcol * 512:(dcol + 1) * 512], ALU.mult)
            ps_free[dcol % 2] = ms_t
            ms_x1 = b.op("vector", "tensor_tensor",
                         x1t[tk % 2][:, dcol * 512:(dcol + 1) * 512],
                         xt[tk % 2][:, dcol * 512:(dcol + 1) * 512], tmul[:], ALU.add)
        x_use[tk % 2] = ms_x1
        g_use[tk % 2] = ms_x1
        b.wait("sync", ms_x1)
        x1_st[tk % 2] = b.dma("sync", io["x1c"][tk * 128:(tk + 1) * 128, :],
                              x1t[tk % 2][:], stx[tk % 2])
    b.wait("sync", b.at(stx[0]))
    b.wait("sync", b.at(stx[1]))
    return


def emit_ln(b, pre, x_ap, xnT, ident, epscol):
    ldx = [b.sem("ldx0"), b.sem("ldx1")]
    ve = b.eng_sem("vector")
    sc = b.eng_sem("scalar")
    mm = b.sem("mm")
    NT = TL // 128
    xt = [b.sb(f"{pre}xt{i}", [128, D], F32) for i in range(2)]
    xnt = [b.sb(f"{pre}xnt{i}", [128, D], BF16) for i in range(2)]
    sq = b.sb(f"{pre}sq", [128, D], BF16)
    stat = b.sb(f"{pre}stat", [128, 8], F32)
    ps_tr = [b.ps(f"{pre}tr{i}", [128, 128], BF16) for i in range(2)]

    xn_done = [None] * NT
    evac_done = [None] * NT
    tr_evac = [None, None]
    prev_ms_xn = None
    for T in range(NT):
        if T >= 2:
            b.wait("sync", xn_done[T - 2])
        ms_load = b.dma("sync", xt[T % 2][:], x_ap[T * 128:(T + 1) * 128, :], ldx[T % 2])
        xb = xt[T % 2]
        xnb = xnt[T % 2]
        b.wait("scalar", ms_load)
        if prev_ms_xn is not None:
            b.wait("scalar", prev_ms_xn)
        ms_ssq = b.op("scalar", "activation", sq[:], xb[:], AF.Square,
                      accum_out=stat[:, 2:3])
        b.wait("vector", ms_load)
        b.op("vector", "reduce_sum", stat[:, 0:1], xb[:], axis=AX.X)
        b.op("vector", "tensor_scalar_mul", stat[:, 1:2], stat[:, 0:1], 1.0 / D)
        b.wait("vector", ms_ssq)
        b.op("vector", "tensor_tensor", stat[:, 5:6], stat[:, 1:2], stat[:, 1:2], ALU.mult)
        b.op("vector", "tensor_scalar", stat[:, 6:7], stat[:, 2:3], 1.0 / D, None, ALU.mult)
        ms_var = b.op("vector", "tensor_sub", stat[:, 6:7], stat[:, 6:7], stat[:, 5:6])
        b.wait("scalar", ms_var)
        b.op("scalar", "activation", stat[:, 3:4], stat[:, 6:7], AF.Ln, bias=epscol[:, 0:1])
        ms_rstd = b.op("scalar", "activation", stat[:, 4:5], stat[:, 3:4], AF.Exp,
                       scale=-0.5)
        b.wait("vector", ms_rstd)
        b.op("vector", "tensor_tensor", stat[:, 7:8], stat[:, 1:2], stat[:, 4:5], ALU.mult)
        b.op("vector", "tensor_scalar_mul", stat[:, 7:8], stat[:, 7:8], -1.0)
        if T >= 2:
            b.wait("vector", evac_done[T - 2])
        ms_xn = b.op("vector", "tensor_scalar", xnb[:], xb[:], stat[:, 4:5],
                     stat[:, 7:8], ALU.mult, ALU.add)
        xn_done[T] = ms_xn
        prev_ms_xn = ms_xn
        b.wait("tensor", ms_xn)
        for ci in range(16):
            b.wait("tensor", tr_evac[ci % 2])
            ms_t = b.op("tensor", "transpose", ps_tr[ci % 2][:],
                        xnb[:, ci * 128:(ci + 1) * 128], ident[:], sem=mm)
            b.wait("vector", ms_t)
            tr_evac[ci % 2] = b.op("vector", "tensor_copy",
                                   xnT[:, ci * TL + T * 128: ci * TL + (T + 1) * 128],
                                   ps_tr[ci % 2][:])
        evac_done[T] = b.at(ve)
    return b.at(ve)


def stage_s4(b, io):
    ldc = b.sem("ldc")
    ve = b.eng_sem("vector")
    sc = b.eng_sem("scalar")
    mm = b.sem("mm")
    wl = [b.sem("wl0"), b.sem("wl1")]
    stk = [b.sem("stk0"), b.sem("stk1")]
    stv = [b.sem("stv0"), b.sem("stv1")]

    wvo_sb = b.sb("s4wvo", [128, 16 * D], BF16)
    ident = b.sb("s4id", [128, 128], BF16)
    epscol = b.sb("s4eps", [128, 1], F32)
    wbuf = [b.sb(f"s4wb{i}", [128, 16 * 128], BF16) for i in range(2)]
    kst = [b.sb(f"s4ks{i}", [128, 512], BF16) for i in range(2)]
    vst = [b.sb(f"s4vs{i}", [128, 512], BF16) for i in range(2)]
    ps_k = [b.ps(f"s4pk{i}", [128, 512], F32) for i in range(2)]
    ps_v = [b.ps(f"s4pv{i}", [128, 512], F32) for i in range(2)]

    ms_id = b.dma("sync", ident[:], io["ident_bf"], ldc)
    b.op("vector", "memset", epscol[:], EPS)
    b.wait("tensor", ms_id)

    b.push_scope()
    wo_full = b.sb("s4wof", [128, 16 * D], BF16)
    ms_wof = b.dma("gpsimd", wo_full[:].rearrange("p (a e) -> p a e", a=16),
                   io["Wo1"].rearrange("(a p) e -> p a e", p=128), b.sem("wlof"))
    wvt_b2 = [b.sb(f"s4wvtb{i}", [128, 16 * 128], BF16) for i in range(2)]
    wvt_use = [None, None]
    psv_free = [None, None]
    b.wait("tensor", ms_wof)
    for dch in range(16):
        b.wait("gpsimd", wvt_use[dch % 2])
        ms_wvt = b.dma("gpsimd", wvt_b2[dch % 2][:].rearrange("p (a e) -> p a e", a=16),
                       io["WvT"][:, dch * 128:(dch + 1) * 128]
                       .rearrange("(a p) e -> p a e", p=128), wl[dch % 2])
        b.wait("tensor", ms_wvt)
        for ec in range(4):
            sl2 = ec % 2
            b.wait("tensor", psv_free[sl2])
            for mc in range(16):
                b.op("tensor", "matmul", ps_v[sl2][:],
                     wvt_b2[dch % 2][:, mc * 128:(mc + 1) * 128],
                     wo_full[:, mc * D + ec * 512:(mc) * D + ec * 512 + 512],
                     start=(mc == 0), stop=(mc == 15), sem=(mm if mc == 15 else None))
            ms_mm = b.at(mm)
            b.wait("vector", ms_mm)
            psv_free[sl2] = b.op("vector", "tensor_copy",
                                 wvo_sb[:, dch * D + ec * 512: dch * D + (ec + 1) * 512],
                                 ps_v[sl2][:])
        wvt_use[dch % 2] = b.at(mm)
    b.wait("tensor", b.at(ve))
    b.pop_scope()

    xnT = b.sb("s4xnT", [128, 16 * TL], BF16)
    emit_ln(b, "s4", io["x1c"], xnT, ident, epscol)

    wuse = [None, None]
    kfree = [None, None]
    b.wait("tensor", b.at(ve))
    for h in range(H):
        b.wait("gpsimd", wuse[h % 2])
        ms_w = b.dma("gpsimd", wbuf[h % 2][:].rearrange("p (a e) -> p a e", a=16),
                     io["Wk1"][:, h * 128:(h + 1) * 128]
                     .rearrange("(a p) e -> p a e", p=128), wl[h % 2])
        b.wait("tensor", ms_w)
        for tc in range(4):
            bb = tc // 2
            pcol = (tc % 2) * 512
            b.wait("tensor", kfree[tc % 2])
            for ci in range(16):
                b.op("tensor", "matmul", ps_k[tc % 2][:],
                     wbuf[h % 2][:, ci * 128:(ci + 1) * 128],
                     xnT[:, ci * TL + tc * 512: ci * TL + (tc + 1) * 512],
                     start=(ci == 0), stop=(ci == 15), sem=(mm if ci == 15 else None))
            ms_mm = b.at(mm)
            b.wait("vector", ms_mm)
            b.wait("vector", kfree[tc % 2])
            ms_ev = b.op("vector", "tensor_copy", kst[tc % 2][:], ps_k[tc % 2][:])
            b.wait("sync", ms_ev)
            ms_st = b.dma("sync",
                          io["Kag"][(bb * H + h) * 128:(bb * H + h + 1) * 128,
                                    pcol:pcol + 512],
                          kst[tc % 2][:], stk[tc % 2])
            kfree[tc % 2] = ms_st
        wuse[h % 2] = b.at(mm)

    vfree = [None, None]
    for ec in range(4):
        for tk in range(16):
            sl2 = tk % 2
            b.wait("tensor", vfree[sl2])
            for ci in range(16):
                b.op("tensor", "matmul", ps_v[sl2][:],
                     xnT[:, ci * TL + tk * 128: ci * TL + (tk + 1) * 128],
                     wvo_sb[:, ci * D + ec * 512: ci * D + (ec + 1) * 512],
                     start=(ci == 0), stop=(ci == 15), sem=(mm if ci == 15 else None))
            ms_v = b.at(mm)
            b.wait("vector", ms_v)
            b.wait("vector", vfree[sl2])
            ms_ev = b.op("vector", "tensor_copy", vst[sl2][:], ps_v[sl2][:])
            b.wait("sync", ms_ev)
            vfree[sl2] = b.dma("sync",
                               io["Vag"][tk * 128:(tk + 1) * 128, ec * 512:(ec + 1) * 512],
                               vst[sl2][:], stv[sl2])
    for sem in (stk[0], stk[1], stv[0], stv[1]):
        b.wait("sync", b.at(sem))
    return


def stage_s5(b, io):
    ldc = b.sem("ldc")
    ve = b.eng_sem("vector")
    sc = b.eng_sem("scalar")
    mm = b.sem("mm")
    wl = [b.sem("wl0"), b.sem("wl1")]
    wl2 = [b.sem("wl20"), b.sem("wl21")]
    ldh = [b.sem("ldh0"), b.sem("ldh1")]
    stq = [b.sem("stq0"), b.sem("stq1")]
    sth = [b.sem("sth0"), b.sem("sth1")]
    stg = [b.sem("stg0"), b.sem("stg1")]

    b.push_scope()
    xnT = b.sb("s5xnT", [128, 16 * TL], BF16)
    ident = b.sb("s5id", [128, 128], BF16)
    epscol = b.sb("s5eps", [128, 1], F32)
    ms_id = b.dma("sync", ident[:], io["ident_bf"], ldc)
    b.op("vector", "memset", epscol[:], EPS)
    b.wait("tensor", ms_id)
    emit_ln(b, "s5", io["x1s"], xnT, ident, epscol)

    wbuf = [b.sb(f"s5wb{i}", [128, 16 * 128], BF16) for i in range(2)]
    qst = [b.sb(f"s5qs{i}", [128, 512], BF16) for i in range(2)]
    ps_q = [b.ps(f"s5pq{i}", [128, 512], F32) for i in range(2)]
    wuse = [None, None]
    qfree = [None, None]
    b.wait("tensor", b.at(ve))
    for h in range(H):
        b.wait("gpsimd", wuse[h % 2])
        ms_w = b.dma("gpsimd", wbuf[h % 2][:].rearrange("p (a e) -> p a e", a=16),
                     io["Wq1"][:, h * 128:(h + 1) * 128]
                     .rearrange("(a p) e -> p a e", p=128), wl[h % 2])
        b.wait("tensor", ms_w)
        for tc in range(4):
            b.wait("tensor", qfree[tc % 2])
            for ci in range(16):
                b.op("tensor", "matmul", ps_q[tc % 2][:],
                     wbuf[h % 2][:, ci * 128:(ci + 1) * 128],
                     xnT[:, ci * TL + tc * 512: ci * TL + (tc + 1) * 512],
                     start=(ci == 0), stop=(ci == 15), sem=(mm if ci == 15 else None))
            ms_mm = b.at(mm)
            b.wait("vector", ms_mm)
            b.wait("vector", qfree[tc % 2])
            ms_ev = b.op("vector", "tensor_copy", qst[tc % 2][:], ps_q[tc % 2][:])
            b.wait("sync", ms_ev)
            qfree[tc % 2] = b.dma("sync",
                                  io["Q1T"][h * 128:(h + 1) * 128,
                                            tc * 512:(tc + 1) * 512],
                                  qst[tc % 2][:], stq[tc % 2])
        wuse[h % 2] = b.at(mm)

    wub = [b.sb(f"s5wu{i}", [128, 16 * 512], BF16) for i in range(2)]
    usc = b.sb("s5usc", [128, 64], F32)
    hline = [b.sb(f"s5hl{i}", [128, TL], BF16) for i in range(2)]
    ps_u = [b.ps(f"s5pu{i}", [128, 512], F32) for i in range(2)]
    ms_usc = b.dma("sync", usc[:], io["usc1_p"], ldc)
    b.wait("scalar", ms_usc)
    wu_use = [None, None]
    hl_store = [None, None]
    ps_u_free = [None, None]
    for sl in range(16):
        b.wait("gpsimd", wu_use[sl % 2])
        ms_wu = b.dma("gpsimd", wub[sl % 2][:].rearrange("p (a c) -> p a c", a=16),
                      io["up_wT1"][:, sl * 512:(sl + 1) * 512]
                      .rearrange("(a p) c -> p a c", p=128), wl2[sl % 2])
        b.wait("scalar", ms_wu)
        ms_sgn = b.op("scalar", "activation", wub[sl % 2][:], wub[sl % 2][:], AF.Sign)
        b.wait("tensor", ms_sgn)
        for isub in range(4):
            ic = sl * 4 + isub
            hl = hline[ic % 2]
            for tcol in range(4):
                pslot = tcol % 2
                b.wait("tensor", ps_u_free[pslot])
                for ci in range(16):
                    b.op("tensor", "matmul", ps_u[pslot][:],
                         wub[sl % 2][:, ci * 512 + isub * 128: ci * 512 + (isub + 1) * 128],
                         xnT[:, ci * TL + tcol * 512: ci * TL + (tcol + 1) * 512],
                         start=(ci == 0), stop=(ci == 15), sem=(mm if ci == 15 else None))
                ms_u = b.at(mm)
                b.wait("scalar", ms_u)
                if tcol == 0:
                    b.wait("scalar", hl_store[ic % 2])
                ms_h = b.op("scalar", "activation", hl[:, tcol * 512:(tcol + 1) * 512],
                            ps_u[pslot][:], AF.Relu, scale=usc[:, ic:ic + 1])
                ps_u_free[pslot] = ms_h
            b.wait("sync", ms_h)
            hl_store[ic % 2] = b.dma(
                "sync", io["h1_hbm"][ic * 128:(ic + 1) * 128, :], hl[:], sth[ic % 2])
        wu_use[sl % 2] = b.at(mm)
    b.wait("sync", b.at(sth[0]))
    b.wait("sync", b.at(sth[1]))
    b.barrier()
    b.pop_scope()

    b.push_scope()
    y_acc = b.sb("s5y", [128, 16 * 1024], F32)
    dsc = b.sb("s5dsc", [128, D], F32)
    wdb = [b.sb(f"s5wd{i}", [128, 8 * 1024], BF16) for i in range(2)]
    hb = [b.sb(f"s5hb{i}", [128, 8 * TL], BF16) for i in range(2)]
    ty = b.sb("s5ty", [128, 1024], F32)
    g1f = [b.sb(f"s5g1{i}", [128, 1024], F32) for i in range(2)]
    osc = b.sb("s5osc", [128, 1], F32)
    ps_y = [b.ps(f"s5py{i}", [128, 1024], F32) for i in range(2)]

    b.dma("sync", dsc[:], io["dsc1_b"], ldc)
    ms_osc = b.dma("sync", osc[:], io["osc_c"], ldc)
    b.wait("vector", ms_osc)
    ms_dsc = b.op("vector", "tensor_scalar", dsc[:], dsc[:], osc[:, 0:1], None, ALU.mult)

    wd_use = [None, None]
    h_use = [None, None]
    g1_store = [None, None]
    ps_y_free = [None, None]
    for blk in range(2):
        for mac in range(8):
            b.wait("gpsimd", wd_use[mac % 2])
            ms_wd = b.dma("gpsimd", wdb[mac % 2][:].rearrange("p (a c) -> p a c", a=8),
                          io["down_wT1"][mac * 1024:(mac + 1) * 1024,
                                         blk * 1024:(blk + 1) * 1024]
                          .rearrange("(a p) c -> p a c", p=128), wl2[mac % 2])
            b.wait("scalar", ms_wd)
            ms_sgn = b.op("scalar", "activation", wdb[mac % 2][:], wdb[mac % 2][:],
                          AF.Sign)
            b.wait("sync", h_use[mac % 2])
            ms_h = b.dma("sync", hb[mac % 2][:].rearrange("p (a c) -> p a c", a=8),
                         io["h1_hbm"][mac * 1024:(mac + 1) * 1024, :]
                         .rearrange("(a p) c -> p a c", p=128), ldh[mac % 2])
            b.wait("tensor", ms_sgn)
            b.wait("tensor", ms_h)
            for tk in range(16):
                yslot = tk % 2
                b.wait("tensor", ps_y_free[yslot])
                for hc in range(2):
                    for ic in range(8):
                        b.op("tensor", "matmul", ps_y[yslot][:, hc * 512:(hc + 1) * 512],
                             hb[mac % 2][:, ic * TL + tk * 128: ic * TL + (tk + 1) * 128],
                             wdb[mac % 2][:, ic * 1024 + hc * 512: ic * 1024 + (hc + 1) * 512],
                             start=(ic == 0), stop=(ic == 7),
                             sem=(mm if (ic == 7 and hc == 1) else None))
                ms_y = b.at(mm)
                b.wait("vector", ms_y)
                if mac == 0:
                    ms_ya = b.op("vector", "tensor_copy",
                                 y_acc[:, tk * 1024:(tk + 1) * 1024], ps_y[yslot][:])
                else:
                    ms_ya = b.op("vector", "tensor_tensor",
                                 y_acc[:, tk * 1024:(tk + 1) * 1024],
                                 y_acc[:, tk * 1024:(tk + 1) * 1024], ps_y[yslot][:], ALU.add)
                ps_y_free[yslot] = ms_ya
            wd_use[mac % 2] = b.at(mm)
            h_use[mac % 2] = b.at(mm)
        for tk in range(16):
            b.wait("vector", g1_store[tk % 2])
            ms_y2 = b.op("vector", "tensor_tensor", g1f[tk % 2][:],
                         y_acc[:, tk * 1024:(tk + 1) * 1024],
                         dsc[:, blk * 1024:(blk + 1) * 1024], ALU.mult)
            b.wait("scalar", ms_y2)
            ms_g1 = b.op("scalar", "activation", g1f[tk % 2][:], g1f[tk % 2][:], AF.Silu)
            b.wait("sync", ms_g1)
            g1_store[tk % 2] = b.dma(
                "sync", io["g1"][tk * 128:(tk + 1) * 128, blk * 1024:(blk + 1) * 1024],
                g1f[tk % 2][:], stg[tk % 2])
    b.wait("sync", b.at(stg[0]))
    b.wait("sync", b.at(stg[1]))
    b.wait("sync", b.at(stq[0]))
    b.wait("sync", b.at(stq[1]))
    b.pop_scope()
    return


def stage_s6(b, io):
    if "ccK" in b._semcache:
        for eng in ("sync", "gpsimd", "tensor"):
            b.wait(eng, (b.sem("ccK"), 1))
            b.wait(eng, (b.sem("ccV"), 1))
    ldc = b.sem("ldc")
    ldk = [b.sem("ldk0"), b.sem("ldk1")]
    ldv = b.sem("ldv0")
    ldq2 = b.sem("ldq2")
    ldw = b.sem("ldw0")
    ldxg = b.sem("ldxg")
    ve = b.eng_sem("vector")
    sc = b.eng_sem("scalar")
    mm = b.sem("mm")
    sto = [b.sem("sto0"), b.sem("sto1")]

    maskc = b.sb("s6mask", [128, 1024], F32)
    ident = b.sb("s6id", [128, 128], BF16)
    qg = b.sb("s6qg", [128, 16 * 512], BF16)
    ka = [b.sb(f"s6ka{i}", [128, 16 * 1024], BF16) for i in range(2)]
    junk = b.sb("s6junk", [128, 4 * 1024], BF16)
    pline = b.sb("s6pl", [128, 1024], BF16)
    PT = b.sb("s6PT", [128, 8 * 512], BF16)
    vb = b.sb("s6vb", [128, 4 * D], BF16)
    nm = b.sb("s6nm", [128, 64], F32)
    dd = b.sb("s6dd", [128, 64], F32)
    bias = b.sb("s6bias", [128, 64], F32)
    sm1 = b.sb("s6sm1", [128, 8], F32)
    cf = b.sb("s6cf", [128, 2], F32)
    xg1 = b.sb("s6x1", [128, D], F32)
    gg1 = b.sb("s6g1", [128, D], F32)
    outt = [b.sb(f"s6out{i}", [128, D], F32) for i in range(2)]


    ms_mask = b.dma("sync", maskc[:], io["l1mask"], ldc)
    ms_id = b.dma("sync", ident[:], io["ident_bf"], ldc)
    b.wait("vector", ms_mask)
    b.wait("tensor", ms_id)

    ka_use = [None, None]
    v_use = None
    out_st = [None, None]
    pv_done = None
    wo_use = None
    for bb in range(B):
        for gg2 in range(2):
            blocks = [4 * gg2 + jj for jj in range(4)]
            jmax = blocks[-1]
            b.wait("sync", wo_use)
            ms_qg = b.dma("sync", qg[:].rearrange("p (h c) -> p h c", h=16),
                          io["Q1T"][:, bb * SL + gg2 * 512: bb * SL + (gg2 + 1) * 512]
                          .rearrange("(h p) c -> p h c", p=128), ldq2)
            b.wait("tensor", ms_qg)
            b.cur_stage = f"s6pa{bb}{gg2}"
            b.op("vector", "memset", nm[:], BIG)
            b.op("vector", "memset", dd[:], 0.0)
            b.push_scope()
            ps_sa = [b.ps(f"s6psa{bb}{gg2}{i}", [128, 1024], F32) for i in range(2)]
            ps_free = [None, None]
            pend_d = None
            it = 0
            for u in range(jmax + 1):
                b.wait("gpsimd", ka_use[u % 2])
                ms_k = b.dma("gpsimd", ka[u % 2][:].rearrange("p (h c) -> p h c", h=16),
                             io["Kall"].rearrange("(r n) c -> r n c", r=NCORE)
                             [u, bb * 2048:(bb + 1) * 2048, :]
                             .rearrange("(h p) c -> p h c", p=128), ldk[u % 2])
                b.wait("tensor", ms_k)
                for jj, j in enumerate(blocks):
                    if u > j:
                        continue
                    diag = (u == j)
                    for h in range(H):
                        col = jj * 16 + h
                        sl2 = it % 2
                        it += 1
                        sb_ = sl2 * 4
                        b.wait("tensor", ps_free[sl2])
                        for half in range(2):
                            b.op("tensor", "matmul",
                                 ps_sa[sl2][:, half * 512:(half + 1) * 512],
                                 qg[:, h * 512 + jj * 128: h * 512 + (jj + 1) * 128],
                                 ka[u % 2][:, h * 1024 + half * 512: h * 1024 + (half + 1) * 512],
                                 start=True, stop=True,
                                 sem=(mm if half == 1 else None))
                        ms_s = b.at(mm)
                        b.wait("vector", ms_s)
                        if diag:
                            b.op("vector", "tensor_tensor", ps_sa[sl2][:], ps_sa[sl2][:],
                                 maskc[:], ALU.add)
                        b.op("vector", "reduce_max", sm1[:, sb_:sb_ + 1], ps_sa[sl2][:],
                             axis=AX.X, negate=True)
                        if u == 0:
                            ms_nm = b.op("vector", "tensor_scalar",
                                         nm[:, col:col + 1], sm1[:, sb_:sb_ + 1],
                                         ASCALE, None, ALU.mult)
                        else:
                            b.op("vector", "tensor_scalar", sm1[:, sb_ + 1:sb_ + 2],
                                 sm1[:, sb_:sb_ + 1], ASCALE, nm[:, col:col + 1],
                                 ALU.mult, ALU.subtract)
                            b.op("vector", "tensor_scalar_min", sm1[:, sb_ + 2:sb_ + 3],
                                 sm1[:, sb_ + 1:sb_ + 2], 0.0)
                            ms_nm = b.op("vector", "tensor_add", nm[:, col:col + 1],
                                         nm[:, col:col + 1], sm1[:, sb_ + 2:sb_ + 3])
                        b.wait("scalar", ms_nm)
                        if u > 0:
                            b.op("scalar", "activation", cf[:, sl2:sl2 + 1],
                                 sm1[:, sb_ + 2:sb_ + 3], AF.Exp)
                        ms_exp = b.op("scalar", "activation",
                                      junk[:, sl2 * 1024:(sl2 + 1) * 1024],
                                      ps_sa[sl2][:], AF.Exp, scale=ASCALE,
                                      bias=nm[:, col:col + 1],
                                      accum_out=sm1[:, sb_ + 3:sb_ + 4])
                        ps_free[sl2] = ms_exp
                        if pend_d is not None:
                            pend_d()
                        def _upd(u=u, col=col, sl2=sl2, sb_=sb_, ms_exp=ms_exp):
                            b.wait("vector", ms_exp)
                            if u == 0:
                                b.op("vector", "tensor_copy", dd[:, col:col + 1],
                                     sm1[:, sb_ + 3:sb_ + 4])
                            else:
                                b.op("vector", "scalar_tensor_tensor",
                                     dd[:, col:col + 1], dd[:, col:col + 1],
                                     cf[:, sl2:sl2 + 1], sm1[:, sb_ + 3:sb_ + 4],
                                     ALU.mult, ALU.add)
                        pend_d = _upd
                ka_use[u % 2] = b.at(mm)
            if pend_d is not None:
                pend_d()
                pend_d = None
            b.wait("tensor", b.at(sc))
            b.pop_scope()
            b.wait("scalar", b.at(ve))
            ms_ln = b.op("scalar", "activation", bias[:], dd[:], AF.Ln)
            b.wait("vector", ms_ln)
            ms_bias = b.op("vector", "tensor_sub", bias[:], nm[:], bias[:])
            b.cur_stage = f"s6pb{bb}{gg2}"
            b.push_scope()
            o2T = b.sb(f"s6o2T{bb}{gg2}", [128, 16 * 512], F32)
            ps_sa = [b.ps(f"s6psb{bb}{gg2}{i}", [128, 1024], F32) for i in range(2)]
            ps_tr = [b.ps(f"s6ptr{bb}{gg2}{i}", [128, 128], BF16) for i in range(2)]
            ps_pv = [b.ps(f"s6ppv{bb}{gg2}{i}", [128, 512], F32) for i in range(2)]
            b.op("vector", "memset", o2T[:], 0.0)
            tr_free = [None, None]
            ps_pv_free = [None, None]
            ps_free = [None, None]
            junk_free = [None, None, None, None]
            it = 0
            for u in range(jmax + 1):
                b.wait("gpsimd", ka_use[u % 2])
                ms_k = b.dma("gpsimd", ka[u % 2][:].rearrange("p (h c) -> p h c", h=16),
                             io["Kall"].rearrange("(r n) c -> r n c", r=NCORE)
                             [u, bb * 2048:(bb + 1) * 2048, :]
                             .rearrange("(h p) c -> p h c", p=128), ldk[u % 2])
                b.wait("tensor", ms_k)
                b.wait("vector", pv_done)
                pt_written = None
                for jj, j in enumerate(blocks):
                    if u > j:
                        ms_z = b.op("vector", "memset",
                                    PT[:].rearrange("p (a c) -> p a c", a=8)
                                    [:, :, jj * 128:(jj + 1) * 128], 0.0)
                        pt_written = ms_z
                        continue
                    diag = (u == j)
                    for h in range(H):
                        col = jj * 16 + h
                        sl2 = it % 2
                        it += 1
                        b.wait("tensor", ps_free[sl2])
                        for half in range(2):
                            b.op("tensor", "matmul",
                                 ps_sa[sl2][:, half * 512:(half + 1) * 512],
                                 qg[:, h * 512 + jj * 128: h * 512 + (jj + 1) * 128],
                                 ka[u % 2][:, h * 1024 + half * 512: h * 1024 + (half + 1) * 512],
                                 start=True, stop=True,
                                 sem=(mm if half == 1 else None))
                        ms_s = b.at(mm)
                        if diag:
                            b.wait("vector", ms_s)
                            b.op("vector", "tensor_tensor", ps_sa[sl2][:], ps_sa[sl2][:],
                                 maskc[:], ALU.add)
                            b.wait("scalar", b.at(ve))
                        b.wait("scalar", ms_s)
                        if h == 0:
                            dst = pline[:]
                            b.wait("scalar", b.at(ve))
                        else:
                            jsl = h % 4
                            dst = junk[:, jsl * 1024:(jsl + 1) * 1024]
                            b.wait("scalar", junk_free[jsl])
                        ms_exp = b.op("scalar", "activation", dst, ps_sa[sl2][:],
                                      AF.Exp, scale=ASCALE, bias=bias[:, col:col + 1])
                        ps_free[sl2] = ms_exp
                        if h > 0:
                            b.wait("vector", ms_exp)
                            junk_free[jsl] = b.op("vector", "tensor_tensor", pline[:],
                                                  pline[:],
                                                  junk[:, jsl * 1024:(jsl + 1) * 1024],
                                                  ALU.add)
                    ms_pl = b.at(ve)
                    b.wait("tensor", ms_pl)
                    for kt in range(8):
                        b.wait("tensor", tr_free[kt % 2])
                        ms_tr = b.op("tensor", "transpose", ps_tr[kt % 2][:],
                                     pline[:, kt * 128:(kt + 1) * 128],
                                     ident[:], sem=mm)
                        b.wait("vector", ms_tr)
                        tr_free[kt % 2] = b.op(
                            "vector", "tensor_copy",
                            PT[:, kt * 512 + jj * 128: kt * 512 + (jj + 1) * 128],
                            ps_tr[kt % 2][:])
                    pt_written = b.at(ve)
                b.wait("tensor", pt_written)
                for hu in range(2):
                    b.wait("sync", v_use)
                    ms_v = b.dma("sync", vb[:].rearrange("p (a c) -> p a c", a=4),
                                 io["Vall"][u * TL + bb * SL + hu * 512:
                                            u * TL + bb * SL + (hu + 1) * 512, :]
                                 .rearrange("(a p) c -> p a c", p=128), ldv)
                    b.wait("tensor", ms_v)
                    for dch in range(16):
                        sl2 = dch % 2
                        b.wait("tensor", ps_pv_free[sl2])
                        for kt in range(4):
                            b.op("tensor", "matmul", ps_pv[sl2][:],
                                 vb[:, kt * D + dch * 128: kt * D + (dch + 1) * 128],
                                 PT[:, (hu * 4 + kt) * 512:(hu * 4 + kt + 1) * 512],
                                 start=(kt == 0), stop=(kt == 3),
                                 sem=(mm if kt == 3 else None))
                        ms_pv = b.at(mm)
                        b.wait("vector", ms_pv)
                        ms_acc = b.op("vector", "tensor_tensor",
                                      o2T[:, dch * 512:(dch + 1) * 512],
                                      o2T[:, dch * 512:(dch + 1) * 512],
                                      ps_pv[sl2][:], ALU.add)
                        ps_pv_free[sl2] = ms_acc
                    v_use = b.at(mm)
                pv_done = b.at(ve)
                ka_use[u % 2] = b.at(mm)
            b.cur_stage = f"s6wo{bb}{gg2}"
            rowbase = bb * SL + gg2 * 512
            b.push_scope()
            outt_q = [b.sb(f"s6oq{bb}{gg2}0", [128, D], F32)] * 2
            prev_out_store = None
            g1q = b.sb(f"s6gq{bb}{gg2}", [128, D], F32)
            s1bf = b.sb(f"s6s1b{bb}{gg2}", [128, 2 * 128], BF16)
            tmpc = b.sb(f"s6tc{bb}{gg2}", [128, 2 * 128], F32)
            tr_free2 = [None, None]
            for qs in range(4):
                rows = slice(rowbase + qs * 128, rowbase + (qs + 1) * 128)
                b.wait("sync", out_st[0])
                b.wait("sync", out_st[1])
                b.wait("sync", prev_out_store)
                ms_x = b.dma("sync", outt_q[qs % 2][:], io["x1s"][rows, :], ldxg)
                ms_g = b.dma("sync", g1q[:], io["g1"][rows, :],
                             b.sem("ldg2"))
                b.wait("vector", ms_x)
                b.wait("vector", ms_g)
                for dch in range(16):
                    csl = dch % 2
                    ms_cast = b.op("vector", "tensor_copy",
                                   s1bf[:, csl * 128:(csl + 1) * 128],
                                   o2T[:, dch * 512 + qs * 128: dch * 512 + (qs + 1) * 128])
                    tsl = dch % 2
                    b.wait("tensor", ms_cast)
                    b.wait("tensor", tr_free2[tsl])
                    ms_tr = b.op("tensor", "transpose", ps_tr[tsl][:],
                                 s1bf[:, csl * 128:(csl + 1) * 128],
                                 ident[:], sem=mm)
                    b.wait("vector", ms_tr)
                    b.op("vector", "tensor_tensor", tmpc[:, tsl * 128:(tsl + 1) * 128],
                         ps_tr[tsl][:], g1q[:, dch * 128:(dch + 1) * 128], ALU.mult)
                    tr_free2[tsl] = b.op(
                        "vector", "tensor_add",
                        outt_q[qs % 2][:, dch * 128:(dch + 1) * 128],
                        outt_q[qs % 2][:, dch * 128:(dch + 1) * 128],
                        tmpc[:, tsl * 128:(tsl + 1) * 128])
                ms_comb = b.at(ve)
                b.wait("sync", ms_comb)
                out_st[qs % 2] = b.dma("sync", io["outc"][rows, :],
                                       outt_q[qs % 2][:], sto[qs % 2])
                prev_out_store = out_st[qs % 2]
            wo_use = b.at(mm)
            b.wait("sync", b.at(sto[0]))
            b.wait("sync", b.at(sto[1]))
            b.pop_scope()
            b.wait("tensor", b.at(ve))
            b.pop_scope()
    b.wait("sync", b.at(sto[0]))
    b.wait("sync", b.at(sto[1]))
    return


def stage_cc_m(b, io):
    cc = b.sem("ccM")
    b.op("gpsimd", "collective_compute", "AllGather", ALU.bypass,
         replica_groups=[list(range(NCORE))],
         ins=[io["M_core"]], outs=[io["M_all"]], sem=cc)


def stage_cc_x1(b, io):
    ccx = b.sem("ccX")
    b.op("gpsimd", "collective_compute", "AllGather", ALU.bypass,
         replica_groups=[list(range(NCORE))],
         ins=[io["x1c"]], outs=[io["x1all"]], sem=ccx)


def stage_cc_kv(b, io):
    cck = b.sem("ccK")
    ccv = b.sem("ccV")
    b.op("gpsimd", "collective_compute", "AllGather", ALU.bypass,
         replica_groups=[list(range(NCORE))],
         ins=[io["Kag"]], outs=[io["Kall"]], sem=cck)
    b.op("gpsimd", "collective_compute", "AllGather", ALU.bypass,
         replica_groups=[list(range(NCORE))],
         ins=[io["Vag"]], outs=[io["Vall"]], sem=ccv)


def stage_extract_x1s(b, io):
    ldc = b.sem("ldc")
    lde = b.sem("lde")
    sb_off = b.sb("xoff", [1, 1], I32)
    ms_off = b.dma("sync", sb_off[:], io["coff"], ldc)
    b.wait("gpsimd", ms_off)
    start = b.count[lde]
    b.count[lde] += 16 * (NCORE * B)
    x1all = io["x1all"]
    x1s = io["x1s"]

    ccx = b.sem("ccX")

    def fn(g, sb_off=sb_off, start=start, x1all=x1all, x1s=x1s, ccx=ccx):
        g.wait_ge(ccx, 1)
        off = g.value_load(sb_off[0:1, 0:1])
        cnt = start
        for r in range(NCORE):
            for bb in range(B):
                src = x1all.rearrange("(r b m s) d -> r b m s d",
                                      b=B, m=128, s=8)[r, bb][:, bass.ds(off, 1), :]
                dst = x1s[bb * SL + r * 128: bb * SL + (r + 1) * 128, :]
                if cnt > 0:
                    g.wait_ge(lde, cnt)
                g.dma_start(out=dst, in_=src).then_inc(lde, 16)
                cnt += 16
        g.wait_ge(lde, cnt)
    b.emit("gpsimd", fn)


def run_launch(stages, tensors, in_maps_np, n_cores=NCORE):
    nc = bass.Bass("TRN2", target_bir_lowering=False, debug=False)
    io = {}
    out_names = []
    for name, spec in tensors.items():
        shape, dtype, kind = spec[:3]
        aspace = spec[3] if len(spec) > 3 else "Local"
        t = nc.dram_tensor(name, list(shape), dtype, kind=kind, addr_space=aspace)
        io[name] = t.ap()
        if kind == "ExternalOutput":
            out_names.append(name)
    es = ExitStack()
    with es:
        b = Builder(nc, es)
        cc_names = ("ccM", "ccX", "ccK", "ccV")
        for st in stages:
            b.cur_stage = st.__name__
            b.push_scope()
            st(b, io)
            b.barrier(skip=cc_names)
            b.pop_scope()
        b.finalize()
    res = run_bass_kernel_spmd(nc, in_maps_np, core_ids=list(range(n_cores)))
    return [{n: r[n] for n in out_names} for r in res.results]


def _prep_weights(inputs):
    w = {}
    w["wqT"] = np.ascontiguousarray(np.asarray(inputs["L0_attn_wq"]).T).astype(np.float32)
    w["wkT"] = np.ascontiguousarray(np.asarray(inputs["L0_attn_wk"]).T).astype(np.float32)
    w["wvT"] = np.ascontiguousarray(np.asarray(inputs["L0_attn_wv"]).T).astype(np.float32)
    w["woT"] = np.ascontiguousarray(np.asarray(inputs["L0_attn_wo"]).T).astype(np.float32)
    w["gate_wT"] = np.ascontiguousarray(np.asarray(inputs["L0_ffn_gate_w"]).T).astype(np.int32)
    w["up_wT"] = np.ascontiguousarray(np.asarray(inputs["L0_ffn_up_w"]).T).astype(np.int32)
    w["down_wT"] = np.ascontiguousarray(np.asarray(inputs["L0_ffn_down_w"]).T).astype(np.int32)
    w["gsc_p"] = np.ascontiguousarray(
        np.asarray(inputs["L0_ffn_gate_scale"]).astype(np.float32).reshape(64, 128).T)
    w["usc_p"] = np.ascontiguousarray(
        np.asarray(inputs["L0_ffn_up_scale"]).astype(np.float32).reshape(64, 128).T)
    w["dsc_b"] = np.broadcast_to(
        np.asarray(inputs["L0_ffn_down_scale"]).astype(np.float32), (128, D)).copy()
    w["Wq1"] = np.ascontiguousarray(np.asarray(inputs["L1_attn_W_q"]).astype(np.float32))
    w["Wk1"] = np.ascontiguousarray(np.asarray(inputs["L1_attn_W_k"]).astype(np.float32))
    w["Wv1"] = np.ascontiguousarray(np.asarray(inputs["L1_attn_W_v"]).astype(np.float32))
    w["WvT"] = np.ascontiguousarray(np.asarray(inputs["L1_attn_W_v"]).T).astype(np.float32)
    w["Wo1"] = np.ascontiguousarray(np.asarray(inputs["L1_attn_W_o"]).astype(np.float32))
    w["up_wT1"] = np.ascontiguousarray(np.asarray(inputs["L1_ffn_up_weight"]).T).astype(np.float32)
    w["down_wT1"] = np.ascontiguousarray(np.asarray(inputs["L1_ffn_down_weight"]).T).astype(np.float32)
    w["usc1_p"] = np.ascontiguousarray(
        np.asarray(inputs["L1_ffn_up_scale"]).astype(np.float32).reshape(64, 128).T)
    w["dsc1_b"] = np.broadcast_to(
        np.asarray(inputs["L1_ffn_down_scale"]).astype(np.float32), (128, D)).copy()
    w["osc_c"] = np.broadcast_to(
        np.asarray(inputs["L1_ffn_output_scale"]).astype(np.float32).reshape(1, 1),
        (128, 1)).copy()
    return w


FUSED_TENSORS = {
    "x0c": ([TL, D], F32, "ExternalInput"),
    "coff": ([1, 1], I32, "ExternalInput"),
    "ropecos": ([128, SL], F32, "ExternalInput"),
    "ropesin": ([128, SL], F32, "ExternalInput"),
    "kvinitw": ([128, NCORE * H], F32, "ExternalInput"),
    "l1mask": ([128, 1024], F32, "ExternalInput"),
    "ropeRT_bf": ([128, 128], BF16, "ExternalInput"),
    "ident_bf": ([128, 128], BF16, "ExternalInput"),
    "ldiagT_bf": ([H * 128, 128], BF16, "ExternalInput"),
    "qdecb_bf": ([H * 128, 128], BF16, "ExternalInput"),
    "kdecc": ([128, H], F32, "ExternalInput"),
    "blkc": ([128, H], F32, "ExternalInput"),
    "ones1_bf": ([128, 1], BF16, "ExternalInput"),
    "wqT": ([D, D], F32, "ExternalInput"),
    "wkT": ([D, 512], F32, "ExternalInput"),
    "wvT": ([D, 512], F32, "ExternalInput"),
    "woT": ([D, D], F32, "ExternalInput"),
    "gate_wT": ([D, DFF], I32, "ExternalInput"),
    "up_wT": ([D, DFF], I32, "ExternalInput"),
    "down_wT": ([DFF, D], I32, "ExternalInput"),
    "gsc_p": ([128, 64], F32, "ExternalInput"),
    "usc_p": ([128, 64], F32, "ExternalInput"),
    "dsc_b": ([128, D], F32, "ExternalInput"),
    "Wq1": ([D, D], F32, "ExternalInput"),
    "Wk1": ([D, D], F32, "ExternalInput"),
    "WvT": ([D, D], F32, "ExternalInput"),
    "Wo1": ([D, D], F32, "ExternalInput"),
    "up_wT1": ([D, DFF], F32, "ExternalInput"),
    "down_wT1": ([DFF, D], F32, "ExternalInput"),
    "usc1_p": ([128, 64], F32, "ExternalInput"),
    "dsc1_b": ([128, D], F32, "ExternalInput"),
    "osc_c": ([128, 1], F32, "ExternalInput"),
    "h_hbm": ([DFF, TL], BF16, "Internal"),
    "xnT": ([D, TL], BF16, "Internal"),
    "qT": ([D, TL], BF16, "Internal"),
    "kT": ([512, TL], BF16, "Internal"),
    "vtok": ([TL, 512], BF16, "Internal"),
    "o_loc": ([D, TL], BF16, "Internal"),
    "M_core": ([2 * H * 128, 128], F32, "Internal"),
    "M_all": ([NCORE * 2 * H * 128, 128], F32, "Internal", "Shared"),
    "g0": ([TL, D], F32, "Internal"),
    "x1c": ([TL, D], F32, "Internal"),
    "x1all": ([NCORE * TL, D], F32, "Internal", "Shared"),
    "x1s": ([TL, D], F32, "Internal"),
    "Kag": ([2 * H * 128, SL], BF16, "Internal"),
    "Vag": ([TL, D], BF16, "Internal"),
    "Kall": ([NCORE * 2 * H * 128, SL], BF16, "Internal", "Shared"),
    "Vall": ([NCORE * TL, D], BF16, "Internal", "Shared"),
    "h1_hbm": ([DFF, TL], BF16, "Internal"),
    "Q1T": ([D, TL], BF16, "Internal"),
    "g1": ([TL, D], F32, "Internal"),
    "outc": ([TL, D], F32, "ExternalOutput"),
}

FUSED_STAGES = [stage_s0, stage_s1, stage_cc_m, stage_s2, stage_s3, stage_cc_x1,
                stage_s4, stage_cc_kv, stage_extract_x1s, stage_s5, stage_s6]


def kernel(**inputs):
    consts = host_constants()
    pc = host_per_core(inputs, consts)
    w = _prep_weights(inputs)
    maps = []
    for c in range(NCORE):
        m = {
            "x0c": pc[c]["x0c"].astype(np.float32),
            "coff": np.array([[c]], dtype=np.int32),
            "ropecos": pc[c]["ropecos"], "ropesin": pc[c]["ropesin"],
            "kvinitw": pc[c]["kvinitw"], "l1mask": pc[c]["l1mask"],
            "ropeRT_bf": consts["ropeRT_bf"], "ident_bf": consts["ident_bf"],
            "ldiagT_bf": consts["ldiagT_bf"], "qdecb_bf": consts["qdecb_bf"],
            "kdecc": consts["kdecc"], "blkc": consts["blkc"],
            "ones1_bf": consts["ones1_bf"],
        }
        for k2 in ("wqT", "wkT", "wvT", "woT", "gate_wT", "up_wT", "down_wT",
                   "gsc_p", "usc_p", "dsc_b", "Wq1", "Wk1", "WvT", "Wo1",
                   "up_wT1", "down_wT1", "usc1_p", "dsc1_b", "osc_c"):
            m[k2] = w[k2]
        maps.append(m)
    res = run_launch(FUSED_STAGES, FUSED_TENSORS, maps)
    out = np.zeros((2, S, D), dtype=np.float32)
    for c in range(NCORE):
        oc = np.asarray(res[c]["outc"])
        idx = 8 * np.arange(SL) + c
        out[0, idx] = oc[0:SL]
        out[1, idx] = oc[SL:TL]
    return out


def _kernel_unfused(**inputs):

    consts = host_constants()
    pc = host_per_core(inputs, consts)
    w = _prep_weights(inputs)

    t1 = {
        "x0c": ([TL, D], F32, "ExternalInput"),
        "wqT": ([D, D], F32, "ExternalInput"),
        "wkT": ([D, 512], F32, "ExternalInput"),
        "wvT": ([D, 512], F32, "ExternalInput"),
        "ropecos": ([128, SL], F32, "ExternalInput"),
        "ropesin": ([128, SL], F32, "ExternalInput"),
        "ropeRT_bf": ([128, 128], BF16, "ExternalInput"),
        "ident_bf": ([128, 128], BF16, "ExternalInput"),
        "ldiagT_bf": ([H * 128, 128], BF16, "ExternalInput"),
        "qdecb_bf": ([H * 128, 128], BF16, "ExternalInput"),
        "kdecc": ([128, H], F32, "ExternalInput"),
        "blkc": ([128, H], F32, "ExternalInput"),
        "gate_wT": ([D, DFF], I32, "ExternalInput"),
        "up_wT": ([D, DFF], I32, "ExternalInput"),
        "down_wT": ([DFF, D], I32, "ExternalInput"),
        "gsc_p": ([128, 64], F32, "ExternalInput"),
        "usc_p": ([128, 64], F32, "ExternalInput"),
        "dsc_b": ([128, D], F32, "ExternalInput"),
        "ones1_bf": ([128, 1], BF16, "ExternalInput"),
        "h_hbm": ([DFF, TL], BF16, "Internal"),
        "xnT": ([D, TL], BF16, "Internal"),
        "qT": ([D, TL], BF16, "ExternalOutput"),
        "kT": ([512, TL], BF16, "Internal"),
        "vtok": ([TL, 512], BF16, "Internal"),
        "o_loc": ([D, TL], BF16, "ExternalOutput"),
        "M_core": ([2 * H * 128, 128], F32, "ExternalOutput"),
        "g0": ([TL, D], F32, "ExternalOutput"),
    }
    maps1 = []
    for c in range(NCORE):
        maps1.append({
            "x0c": pc[c]["x0c"].astype(np.float32),
            "wqT": w["wqT"], "wkT": w["wkT"], "wvT": w["wvT"],
            "ropecos": pc[c]["ropecos"], "ropesin": pc[c]["ropesin"],
            "ropeRT_bf": consts["ropeRT_bf"], "ident_bf": consts["ident_bf"],
            "ldiagT_bf": consts["ldiagT_bf"], "qdecb_bf": consts["qdecb_bf"],
            "kdecc": consts["kdecc"], "blkc": consts["blkc"],
            "gate_wT": w["gate_wT"], "up_wT": w["up_wT"], "down_wT": w["down_wT"],
            "gsc_p": w["gsc_p"], "usc_p": w["usc_p"], "dsc_b": w["dsc_b"],
            "ones1_bf": consts["ones1_bf"],
        })
    res1 = run_launch([stage_s0, stage_s1, stage_s2], t1, maps1)

    M_all = np.concatenate([np.asarray(r["M_core"]) for r in res1], axis=0)

    t2 = {
        "o_loc": ([D, TL], BF16, "ExternalInput"),
        "qT": ([D, TL], BF16, "ExternalInput"),
        "M_all": ([NCORE * 2 * H * 128, 128], F32, "ExternalInput"),
        "kvinitw": ([128, NCORE * H], F32, "ExternalInput"),
        "qdecb_bf": ([H * 128, 128], BF16, "ExternalInput"),
        "blkc": ([128, H], F32, "ExternalInput"),
        "woT": ([D, D], F32, "ExternalInput"),
        "x0c": ([TL, D], F32, "ExternalInput"),
        "g0": ([TL, D], F32, "ExternalInput"),
        "Wk1": ([D, D], F32, "ExternalInput"),
        "Wv1": ([D, D], F32, "ExternalInput"),
        "ident_bf": ([128, 128], BF16, "ExternalInput"),
        "x1c": ([TL, D], F32, "ExternalOutput"),
        "Kag": ([2 * H * 128, SL], BF16, "ExternalOutput"),
        "Vag": ([TL, D], BF16, "ExternalOutput"),
    }
    maps2 = []
    for c in range(NCORE):
        maps2.append({
            "o_loc": np.asarray(res1[c]["o_loc"]),
            "qT": np.asarray(res1[c]["qT"]),
            "M_all": M_all,
            "kvinitw": pc[c]["kvinitw"],
            "qdecb_bf": consts["qdecb_bf"], "blkc": consts["blkc"],
            "woT": w["woT"],
            "x0c": pc[c]["x0c"].astype(np.float32),
            "g0": np.asarray(res1[c]["g0"]),
            "Wk1": w["Wk1"], "Wv1": w["Wv1"], "ident_bf": consts["ident_bf"],
        })
    res2 = run_launch([stage_s3, stage_s4], t2, maps2)

    Kall = np.concatenate([np.asarray(r["Kag"]) for r in res2], axis=0)
    Vall = np.concatenate([np.asarray(r["Vag"]) for r in res2], axis=0)
    x1g = np.zeros((2, S, D), dtype=np.float32)
    for c in range(NCORE):
        x1c = np.asarray(res2[c]["x1c"])
        for bb in range(2):
            x1g[bb, c * SL:(c + 1) * SL] = x1c[bb * SL:(bb + 1) * SL]
    x1s_list = []
    for c in range(NCORE):
        idx = 8 * np.arange(SL) + c
        x1s_list.append(np.ascontiguousarray(
            np.concatenate([x1g[0, idx], x1g[1, idx]], axis=0)))

    t3 = {
        "x1s": ([TL, D], F32, "ExternalInput"),
        "Wq1": ([D, D], F32, "ExternalInput"),
        "up_wT1": ([D, DFF], F32, "ExternalInput"),
        "down_wT1": ([DFF, D], F32, "ExternalInput"),
        "usc1_p": ([128, 64], F32, "ExternalInput"),
        "dsc1_b": ([128, D], F32, "ExternalInput"),
        "osc_c": ([128, 1], F32, "ExternalInput"),
        "ident_bf": ([128, 128], BF16, "ExternalInput"),
        "Kall": ([NCORE * 2 * H * 128, SL], BF16, "ExternalInput"),
        "Vall": ([NCORE * TL, D], BF16, "ExternalInput"),
        "l1mask": ([128, 1024], F32, "ExternalInput"),
        "Wo1": ([D, D], F32, "ExternalInput"),
        "h1_hbm": ([DFF, TL], BF16, "Internal"),
        "Q1T": ([D, TL], BF16, "Internal"),
        "g1": ([TL, D], F32, "Internal"),
        "outc": ([TL, D], F32, "ExternalOutput"),
    }
    maps3 = []
    for c in range(NCORE):
        maps3.append({
            "x1s": x1s_list[c],
            "Wq1": w["Wq1"], "up_wT1": w["up_wT1"], "down_wT1": w["down_wT1"],
            "usc1_p": w["usc1_p"], "dsc1_b": w["dsc1_b"], "osc_c": w["osc_c"],
            "ident_bf": consts["ident_bf"],
            "Kall": Kall, "Vall": Vall,
            "l1mask": pc[c]["l1mask"], "Wo1": w["Wo1"],
        })
    res3 = run_launch([stage_s5, stage_s6], t3, maps3)

    out = np.zeros((2, S, D), dtype=np.float32)
    for c in range(NCORE):
        oc = np.asarray(res3[c]["outc"])
        idx = 8 * np.arange(SL) + c
        out[0, idx] = oc[0:SL]
        out[1, idx] = oc[SL:TL]
    return out


# version 12
# speedup vs baseline: 1.0020x; 1.0020x over previous
from contextlib import ExitStack

import numpy as np
import ml_dtypes

import concourse.bass as bass
import concourse.mybir as mybir
from concourse.bass_utils import run_bass_kernel_spmd

F32 = mybir.dt.float32
BF16 = mybir.dt.bfloat16
I32 = mybir.dt.int32
AF = mybir.ActivationFunctionType
ALU = mybir.AluOpType
AX = mybir.AxisListType

NCORE = 8
B = 2
S = 8192
D = 2048
H, HKV, DH = 16, 4, 128
GRP = H // HKV
DFF = 8192
SL = S // NCORE
TL = B * SL
CH = 128
NCH = SL // CH
EPS = 1e-5
ASCALE = 1.0 / float(np.sqrt(DH))
NEG = -1.0e9

ENGINES = ("sync", "gpsimd", "vector", "scalar", "tensor")
BIG = 3.0e38


class Builder:
    def __init__(self, nc, es):
        self.nc = nc
        self._es = es
        self.q = {e: [] for e in ENGINES}
        self.count = {}
        self._semcache = {}
        self._scopes = []

    def sem(self, name):
        if name in self._semcache:
            return self._semcache[name]
        handle = self._es.enter_context(self.nc.semaphore(name))
        self.count[handle] = 0
        self._semcache[name] = handle
        return handle

    def sb(self, name, shape, dtype):
        es = self._scope if self._scope is not None else self._es
        return es.enter_context(self.nc.sbuf_tensor(name, shape, dtype))

    def ps(self, name, shape, dtype=F32):
        es = self._scope if self._scope is not None else self._es
        return es.enter_context(self.nc.psum_tensor(name, shape, dtype))

    def push_scope(self):
        self._scopes.append(ExitStack())

    def pop_scope(self):
        self._scopes.pop().close()

    @property
    def _scope(self):
        return self._scopes[-1] if self._scopes else None

    def barrier(self, skip=()):
        skiph = {self._semcache[n] for n in skip if n in self._semcache}
        snap = [(sem, cnt) for sem, cnt in self.count.items()
                if cnt > 0 and sem not in skiph]
        for eng in ENGINES:
            for ms in snap:
                self.wait(eng, ms)

    def emit(self, eng, fn):
        self.q[eng].append((getattr(self, "cur_stage", "x"), fn))

    SELF = ("vector", "scalar", "gpsimd")

    def op(self, eng, fname, *args, sem=None, inc=1, **kw):
        if eng in self.SELF and fname not in ("dma_start", "collective_compute"):
            esem = self.sem("E" + eng[0])
            assert sem is None or sem is esem, "use the engine sem on " + eng
            sem = esem
            prev = self.count[sem]
            if prev > 0:
                self.wait(eng, (sem, prev))
        ms = None
        if sem is not None:
            self.count[sem] += inc
            ms = (sem, self.count[sem])

        def fn(E, fname=fname, args=args, kw=kw, sem=sem, inc=inc):
            ins = getattr(E, fname)(*args, **kw)
            if sem is not None:
                ins.then_inc(sem, inc)
        self.emit(eng, fn)
        return ms

    def eng_sem(self, eng):
        return self.sem("E" + eng[0])

    def dma(self, eng, out, in_, sem, **kw):
        prev = self.count[sem]
        if prev > 0:
            self.wait(eng, (sem, prev))
        return self.op(eng, "dma_start", sem=sem, inc=16, out=out, in_=in_, **kw)

    def wait(self, eng, ms):
        if ms is None:
            return
        sem, val = ms

        def fn(E, sem=sem, val=val):
            E.wait_ge(sem, val)
        self.emit(eng, fn)

    def at(self, sem):
        return (sem, self.count[sem])

    def finalize(self):
        block = self._es.enter_context(self.nc.Block())
        nc = self.nc
        for name in ENGINES:
            ops = self.q[name]
            if not ops:
                continue

            def body(E, ops=ops):
                cur = None
                sid = None
                for stg, fn in ops:
                    if stg != cur:
                        if cur is not None:
                            nc.leave_named_scope(cur, sid, False)
                        sid, _ = nc.enter_named_scope(stg, False)
                        cur = stg
                    fn(E)
                if cur is not None:
                    nc.leave_named_scope(cur, sid, False)
            getattr(block, name)(body)


def _bf(x):
    return np.ascontiguousarray(x).astype(ml_dtypes.bfloat16)


def host_constants():
    c = {}
    c["ident_bf"] = _bf(np.eye(128, dtype=np.float32))
    c["ones1_bf"] = _bf(np.ones((128, 1), dtype=np.float32))
    RT = np.zeros((128, 128), dtype=np.float32)
    for d in range(64):
        RT[d + 64, d] = -1.0
    for d in range(64, 128):
        RT[d - 64, d] = 1.0
    c["ropeRT_bf"] = _bf(RT)
    lam = np.exp2(-8.0 * np.arange(1, H + 1, dtype=np.float64) / H)
    r = np.arange(CH, dtype=np.float64)
    dk = np.zeros((H, CH, CH), dtype=np.float64)
    for h in range(H):
        diff = r[:, None] - r[None, :]
        dk[h] = np.where(diff >= 0, np.exp(-lam[h] * diff), 0.0)
    c["ldiagT"] = np.ascontiguousarray(
        dk.transpose(0, 2, 1).astype(np.float32).reshape(H * CH, CH))
    qdec = np.exp(-lam[:, None] * (r + 1.0))
    kdec = np.exp(-lam[:, None] * (CH - 1.0 - r))
    c["qdecb"] = np.broadcast_to(
        qdec[:, None, :], (H, 128, CH)).astype(np.float32).reshape(H * 128, CH).copy()
    c["kdecc"] = np.ascontiguousarray(kdec.T.astype(np.float32))
    c["ldiagT_bf"] = _bf(c["ldiagT"])
    c["qdecb_bf"] = _bf(c["qdecb"])
    c["blkc"] = np.broadcast_to(
        np.exp(-lam * CH).astype(np.float32), (128, H)).copy()
    c["lam"] = lam
    return c


def host_per_core(inputs, consts):
    x = np.asarray(inputs["L0_attn_x"])
    lam = consts["lam"]
    inv_freq = 1.0 / 10000.0 ** (np.arange(0, DH, 2, dtype=np.float64) / DH)
    per_core = []
    for c in range(NCORE):
        m = {}
        sl = slice(c * SL, (c + 1) * SL)
        m["x0c"] = np.ascontiguousarray(np.concatenate([x[0, sl], x[1, sl]], axis=0))
        pos = np.arange(c * SL, (c + 1) * SL, dtype=np.float64)
        fr = pos[None, :] * inv_freq[:, None]
        m["ropecos"] = np.concatenate([np.cos(fr), np.cos(fr)], axis=0).astype(np.float32)
        m["ropesin"] = np.concatenate([np.sin(fr), np.sin(fr)], axis=0).astype(np.float32)
        wk_init = np.zeros((NCORE, H), dtype=np.float64)
        for cp in range(c):
            wk_init[cp] = np.exp(-lam * SL * (c - cp - 1))
        m["kvinitw"] = np.broadcast_to(
            wk_init.astype(np.float32).reshape(1, NCORE * H), (128, NCORE * H)).copy()
        i = np.arange(128)[:, None]
        kk = np.arange(1024)[None, :]
        m["l1mask"] = np.where(kk <= 8 * i + c, 0.0, NEG).astype(np.float32)
        per_core.append(m)
    return per_core


def stage_s0(b, io):
    ldc = b.sem("ldc")
    ldx = [b.sem("ldx0"), b.sem("ldx1")]
    ve = b.eng_sem("vector")
    sc = b.eng_sem("scalar")
    mm = b.sem("mm")
    stq = [b.sem("stq0"), b.sem("stq1")]
    stv = [b.sem("stv0"), b.sem("stv1")]
    sto = b.sem("sto")
    wl = [b.sem("wl0"), b.sem("wl1")]

    NT = TL // 128

    xnT = b.sb("s0xnT", [128, 16 * TL], BF16)
    ident = b.sb("s0id", [128, 128], BF16)
    ropeRT = b.sb("s0RT", [128, 128], BF16)
    cosS = b.sb("s0cos", [128, SL], F32)
    sinS = b.sb("s0sin", [128, SL], F32)
    xt = [b.sb(f"s0xt{i}", [128, D], F32) for i in range(2)]
    xnt = [b.sb(f"s0xnt{i}", [128, D], BF16) for i in range(2)]
    sq = b.sb("s0sq", [128, D], BF16)
    stat = b.sb("s0stat", [128, 8], F32)
    wbuf = [b.sb(f"s0wb{i}", [128, 16 * 128], BF16) for i in range(2)]
    wv_sb = b.sb("s0wv", [128, 16 * 512], BF16)
    qraw = b.sb("s0qraw", [128, 512], BF16)
    t1 = b.sb("s0t1", [128, 512], F32)
    t2 = b.sb("s0t2", [128, 512], F32)
    qfin = [b.sb(f"s0qf{i}", [128, 512], BF16) for i in range(2)]
    vst = [b.sb(f"s0vs{i}", [128, 512], BF16) for i in range(2)]

    epscol = b.sb("s0eps", [128, 1], F32)

    ps_tr = [b.ps(f"s0tr{i}", [128, 128], BF16) for i in range(2)]
    ps_q = b.ps("s0q", [128, 512], F32)
    ps_rot = b.ps("s0rot", [128, 512], F32)
    ps_v = [b.ps(f"s0v{i}", [128, 512], F32) for i in range(2)]

    b.dma("sync", ident[:], io["ident_bf"], ldc)
    b.dma("sync", ropeRT[:], io["ropeRT_bf"], ldc)
    b.dma("sync", cosS[:], io["ropecos"], ldc)
    b.dma("sync", sinS[:], io["ropesin"], ldc)
    ms_consts = b.at(ldc)

    b.op("vector", "memset", epscol[:], EPS, sem=ve)
    xn_done = [None] * NT
    evac_done = [None] * NT
    tr_evac = [None, None]
    prev_ms_xn = None
    for T in range(NT):
        if T >= 2:
            b.wait("sync", xn_done[T - 2])
        ms_load = b.dma("sync", xt[T % 2][:], io["x0c"][T * 128:(T + 1) * 128, :], ldx[T % 2])
        xb = xt[T % 2]
        xnb = xnt[T % 2]
        b.wait("scalar", ms_load)
        if prev_ms_xn is not None:
            b.wait("scalar", prev_ms_xn)
        b.op("scalar", "activation", sq[:], xb[:], AF.Square,
             accum_out=stat[:, 2:3], sem=sc)
        ms_ssq = b.at(sc)
        b.wait("vector", ms_load)
        b.op("vector", "reduce_sum", stat[:, 0:1], xb[:], axis=AX.X)
        b.op("vector", "tensor_scalar_mul", stat[:, 1:2], stat[:, 0:1], 1.0 / D)
        b.wait("vector", ms_ssq)
        b.op("vector", "tensor_tensor", stat[:, 5:6], stat[:, 1:2], stat[:, 1:2], ALU.mult)
        b.op("vector", "tensor_scalar", stat[:, 6:7], stat[:, 2:3], 1.0 / D, None, ALU.mult)
        b.op("vector", "tensor_sub", stat[:, 6:7], stat[:, 6:7], stat[:, 5:6], sem=ve)
        ms_var = b.at(ve)
        b.wait("scalar", ms_var)
        b.op("scalar", "activation", stat[:, 3:4], stat[:, 6:7], AF.Ln, bias=epscol[:, 0:1])
        b.op("scalar", "activation", stat[:, 4:5], stat[:, 3:4], AF.Exp,
             scale=-0.5, sem=sc)
        ms_rstd = b.at(sc)
        b.wait("vector", ms_rstd)
        b.op("vector", "tensor_tensor", stat[:, 7:8], stat[:, 1:2], stat[:, 4:5], ALU.mult)
        b.op("vector", "tensor_scalar_mul", stat[:, 7:8], stat[:, 7:8], -1.0)
        if T >= 2:
            b.wait("vector", evac_done[T - 2])
        b.op("vector", "tensor_scalar", xnb[:], xb[:], stat[:, 4:5], stat[:, 7:8],
             ALU.mult, ALU.add, sem=ve)
        ms_xn = b.at(ve)
        xn_done[T] = ms_xn
        prev_ms_xn = ms_xn
        if T == 0:
            b.wait("tensor", ms_consts)
        b.wait("tensor", ms_xn)
        for ci in range(16):
            b.wait("tensor", tr_evac[ci % 2])
            b.op("tensor", "transpose", ps_tr[ci % 2][:],
                 xnb[:, ci * 128:(ci + 1) * 128], ident[:], sem=mm)
            ms_t = b.at(mm)
            b.wait("vector", ms_t)
            b.op("vector", "tensor_copy",
                 xnT[:, ci * TL + T * 128: ci * TL + (T + 1) * 128],
                 ps_tr[ci % 2][:], sem=ve)
            tr_evac[ci % 2] = b.at(ve)
        evac_done[T] = b.at(ve)

    b.wait("sync", evac_done[-1])
    for ci in range(16):
        b.dma("sync", io["xnT"][ci * 128:(ci + 1) * 128, :],
              xnT[:, ci * TL:(ci + 1) * TL], sto)

    jobs = [(io["wqT"], h * 128, io["qT"]) for h in range(H)] + \
           [(io["wkT"], h * 128, io["kT"]) for h in range(HKV)]

    wuse = [None] * len(jobs)
    ps_q_free = None
    qraw_free = None
    rot_free = None
    qfin_free = [None, None]
    b.wait("tensor", evac_done[-1])
    b.wait("vector", ms_consts)
    for j, (wT, e0, outT) in enumerate(jobs):
        wb = wbuf[j % 2]
        if j >= 2:
            b.wait("gpsimd", wuse[j - 2])
        src = wT[:, e0:e0 + 128].rearrange("(a p) e -> p a e", p=128)
        ms_w = b.dma("gpsimd", wb[:].rearrange("p (a e) -> p a e", a=16), src, wl[j % 2])
        b.wait("tensor", ms_w)
        for tc in range(4):
            cs = slice(tc * 512, (tc + 1) * 512)
            b.wait("tensor", ps_q_free)
            for ci in range(16):
                b.op("tensor", "matmul", ps_q[:], wb[:, ci * 128:(ci + 1) * 128],
                     xnT[:, ci * TL + tc * 512: ci * TL + (tc + 1) * 512],
                     start=(ci == 0), stop=(ci == 15),
                     sem=(mm if ci == 15 else None))
            ms_mm = b.at(mm)
            wuse[j] = ms_mm
            b.wait("scalar", ms_mm)
            b.wait("scalar", qraw_free)
            b.op("scalar", "copy", qraw[:], ps_q[:], sem=sc)
            ms_qraw = b.at(sc)
            ps_q_free = ms_qraw
            b.wait("tensor", ms_qraw)
            b.wait("tensor", rot_free)
            b.op("tensor", "matmul", ps_rot[:], ropeRT[:], qraw[:],
                 start=True, stop=True, sem=mm)
            ms_rot = b.at(mm)
            pc = (tc % 2) * 512
            b.wait("vector", ms_qraw)
            b.op("vector", "tensor_tensor", t1[:], qraw[:], cosS[:, pc:pc + 512], ALU.mult)
            b.wait("vector", ms_rot)
            b.op("vector", "tensor_tensor", t2[:], ps_rot[:], sinS[:, pc:pc + 512],
                 ALU.mult, sem=ve)
            ms_rotdone = b.at(ve)
            qraw_free = ms_rotdone
            rot_free = ms_rotdone
            b.wait("vector", qfin_free[tc % 2])
            b.op("vector", "tensor_add", qfin[tc % 2][:], t1[:], t2[:], sem=ve)
            ms_fin = b.at(ve)
            b.wait("sync", ms_fin)
            ms_st = b.dma("sync", outT[e0:e0 + 128, cs], qfin[tc % 2][:], stq[tc % 2])
            qfin_free[tc % 2] = ms_st

    srcv = io["wvT"].rearrange("(a p) e -> p a e", p=128)
    ms_wv = b.dma("gpsimd", wv_sb[:].rearrange("p (a e) -> p a e", a=16), srcv, wl[0])
    b.wait("tensor", ms_wv)
    ps_v_free = [None, None]
    vst_free = [None, None]
    for tk in range(16):
        b.wait("tensor", ps_v_free[tk % 2])
        for ci in range(16):
            b.op("tensor", "matmul", ps_v[tk % 2][:],
                 xnT[:, ci * TL + tk * 128: ci * TL + (tk + 1) * 128],
                 wv_sb[:, ci * 512:(ci + 1) * 512],
                 start=(ci == 0), stop=(ci == 15), sem=(mm if ci == 15 else None))
        ms_v = b.at(mm)
        b.wait("vector", ms_v)
        b.wait("vector", vst_free[tk % 2])
        b.op("vector", "tensor_copy", vst[tk % 2][:], ps_v[tk % 2][:], sem=ve)
        ms_ev = b.at(ve)
        ps_v_free[tk % 2] = ms_ev
        b.wait("sync", ms_ev)
        ms_stv = b.dma("sync", io["vtok"][tk * 128:(tk + 1) * 128, :], vst[tk % 2][:], stv[tk % 2])
        vst_free[tk % 2] = ms_stv

    for sem in (sto, stq[0], stq[1], stv[0], stv[1]):
        b.wait("sync", b.at(sem))
    return


def stage_s1(b, io):
    ldc = b.sem("ldc")
    ldq = [b.sem("ldq0"), b.sem("ldq1")]
    ve = b.eng_sem("vector")
    sc = b.eng_sem("scalar")
    mm = b.sem("mm")
    sto1 = b.sem("sto1")

    kT_sb = b.sb("s1kT", [128, 4 * TL], BF16)
    vtok_sb = b.sb("s1v", [128, 16 * 512], BF16)
    ldiagT = b.sb("s1diag", [128, 16 * 128], BF16)
    qdecb = b.sb("s1qdec", [128, 16 * 128], BF16)
    kdecc = b.sb("s1kdec", [128, 16], F32)
    blkc = b.sb("s1blk", [128, 16], F32)
    ident = b.sb("s1id", [128, 128], BF16)
    KV = b.sb("s1KV", [128, 32 * 128], F32)
    o_sb = b.sb("s1o", [128, 16 * TL], BF16)
    qtl = [b.sb(f"s1q{i}", [128, SL], BF16) for i in range(2)]
    ktok = b.sb("s1ktok", [128, NCH * 128], BF16)
    kvbf = b.sb("s1kvbf", [128, 128], BF16)
    qsc = b.sb("s1qsc", [128, 128], BF16)
    ksc = b.sb("s1ksc", [128, 128], BF16)
    mstage = [b.sb(f"s1mst{i}", [128, 128], F32) for i in range(2)]

    ps_o = b.ps("s1po", [128, 128], F32)
    ps_s = b.ps("s1ps", [128, 128], F32)
    ps_m = b.ps("s1pm", [128, 128], F32)
    ps_tr = [b.ps(f"s1ptr{i}", [128, 128], BF16) for i in range(2)]

    b.dma("sync", kT_sb[:].rearrange("p (a c) -> p a c", a=4),
          io["kT"].rearrange("(a p) c -> p a c", p=128), ldc)
    b.dma("sync", vtok_sb[:].rearrange("p (a c) -> p a c", a=16),
          io["vtok"].rearrange("(a p) c -> p a c", p=128), ldc)
    b.dma("sync", ldiagT[:].rearrange("p (a c) -> p a c", a=16),
          io["ldiagT_bf"].rearrange("(a p) c -> p a c", p=128), ldc)
    b.dma("sync", qdecb[:].rearrange("p (a c) -> p a c", a=16),
          io["qdecb_bf"].rearrange("(a p) c -> p a c", p=128), ldc)
    b.dma("sync", kdecc[:], io["kdecc"], ldc)
    b.dma("sync", blkc[:], io["blkc"], ldc)
    b.dma("sync", ident[:], io["ident_bf"], ldc)
    ms_c = b.at(ldc)
    ms_kv0 = b.op("vector", "memset", KV[:], 0.0)
    b.wait("tensor", ms_c)

    tr_free = [None, None]
    q_use = [None, None]
    mst_st = [None, None]
    hidx = 0
    for bb in range(B):
        for kvh in range(HKV):
            for t in range(NCH):
                b.wait("tensor", tr_free[t % 2])
                ms_tr = b.op("tensor", "transpose", ps_tr[t % 2][:],
                             kT_sb[:, kvh * TL + bb * SL + t * 128:
                                   kvh * TL + bb * SL + (t + 1) * 128],
                             ident[:], sem=mm)
                b.wait("vector", ms_tr)
                ms_ev = b.op("vector", "tensor_copy",
                             ktok[:, t * 128:(t + 1) * 128], ps_tr[t % 2][:])
                tr_free[t % 2] = ms_ev
            ms_ktok = tr_free[(NCH - 1) % 2]
            b.wait("tensor", ms_ktok)
            for hq in range(GRP):
                h = kvh * GRP + hq
                bh = bb * H + h
                slot = hidx % 2
                b.wait("sync", q_use[slot])
                ms_q = b.dma("sync", qtl[slot][:],
                             io["qT"][h * 128:(h + 1) * 128, bb * SL:(bb + 1) * SL],
                             ldq[slot])
                b.wait("vector", ms_q)
                b.wait("tensor", ms_q)
                for t in range(NCH):
                    qc = qtl[slot][:, t * 128:(t + 1) * 128]
                    kc = kT_sb[:, kvh * TL + bb * SL + t * 128:
                               kvh * TL + bb * SL + (t + 1) * 128]
                    vc = vtok_sb[:, (bb * NCH + t) * 512 + kvh * 128:
                                 (bb * NCH + t) * 512 + (kvh + 1) * 128]
                    ktc = ktok[:, t * 128:(t + 1) * 128]
                    ms_kvbf = b.op("vector", "tensor_copy", kvbf[:],
                                   KV[:, bh * 128:(bh + 1) * 128])
                    ms_qsc = b.op("vector", "tensor_tensor", qsc[:], qc,
                                  qdecb[:, h * 128:(h + 1) * 128], ALU.mult)
                    ms_ksc = b.op("vector", "tensor_scalar", ksc[:], ktc,
                                  kdecc[:, h:h + 1], None, ALU.mult)
                    b.wait("tensor", ms_qsc)
                    b.op("tensor", "matmul", ps_o[:], kvbf[:], qsc[:],
                         start=True, stop=False)
                    ms_sT = b.op("tensor", "matmul", ps_s[:], kc, qc,
                                 start=True, stop=True, sem=mm)
                    b.wait("vector", ms_sT)
                    ms_stl = b.op("vector", "tensor_tensor", qsc[:],
                                  ps_s[:], ldiagT[:, h * 128:(h + 1) * 128], ALU.mult)
                    b.wait("tensor", ms_stl)
                    ms_o = b.op("tensor", "matmul", ps_o[:], vc, qsc[:],
                                start=False, stop=True, sem=mm)
                    b.wait("tensor", ms_ksc)
                    ms_m = b.op("tensor", "matmul", ps_m[:], ksc[:], vc,
                                start=True, stop=True, sem=mm)
                    b.wait("vector", ms_o)
                    ms_oe = b.op("vector", "tensor_copy",
                                 o_sb[:, h * TL + bb * SL + t * 128:
                                      h * TL + bb * SL + (t + 1) * 128], ps_o[:])
                    b.wait("vector", ms_m)
                    ms_upd = b.op("vector", "scalar_tensor_tensor",
                                  KV[:, bh * 128:(bh + 1) * 128],
                                  KV[:, bh * 128:(bh + 1) * 128],
                                  blkc[:, h:h + 1], ps_m[:], ALU.mult, ALU.add)
                q_use[slot] = ms_m
                b.wait("vector", mst_st[hidx % 2])
                ms_ms = b.op("vector", "tensor_copy", mstage[hidx % 2][:],
                             KV[:, bh * 128:(bh + 1) * 128])
                b.wait("sync", ms_ms)
                mst_st[hidx % 2] = b.dma(
                    "sync", io["M_core"][bh * 128:(bh + 1) * 128, :],
                    mstage[hidx % 2][:], sto1)
                hidx += 1

    b.wait("sync", b.at(ve))
    for h in range(H):
        b.dma("sync", io["o_loc"][h * 128:(h + 1) * 128, :],
              o_sb[:, h * TL:(h + 1) * TL], sto1)
    b.wait("sync", b.at(sto1))
    return


def stage_s2(b, io):
    ldc = b.sem("ldc")
    ve = b.eng_sem("vector")
    sc = b.eng_sem("scalar")
    mm = b.sem("mm")
    wlg = [b.sem("wlg0"), b.sem("wlg1")]
    wlu = [b.sem("wlu0"), b.sem("wlu1")]
    wld = [b.sem("wld0"), b.sem("wld1")]
    ldh = [b.sem("ldh0"), b.sem("ldh1")]
    sth = [b.sem("sth0"), b.sem("sth1")]
    stg = [b.sem("stg0"), b.sem("stg1")]

    b.push_scope()
    xnT = b.sb("s2xnT", [128, 16 * TL], BF16)
    c128 = b.sb("s2c128", [128, 128], BF16)
    c1 = b.sb("s2c1", [128, TL], F32)
    gsc = b.sb("s2gsc", [128, 64], F32)
    usc = b.sb("s2usc", [128, 64], F32)
    wgb = [b.sb(f"s2wg{i}", [128, 16 * 512], BF16) for i in range(2)]
    wub = [b.sb(f"s2wu{i}", [128, 16 * 512], BF16) for i in range(2)]
    tg = b.sb("s2tg", [128, 2 * 512], F32)
    tu = b.sb("s2tu", [128, 2 * 512], F32)
    sg = b.sb("s2sg", [128, 2 * 512], BF16)
    hline = [b.sb(f"s2hl{i}", [128, TL], BF16) for i in range(2)]
    ps_c = b.ps("s2pc", [128, 512], F32)
    ps_g = [b.ps(f"s2pg{i}", [128, 512], F32) for i in range(2)]
    ps_u = [b.ps(f"s2pu{i}", [128, 512], F32) for i in range(2)]

    ms_xn = b.dma("sync", xnT[:].rearrange("p (a c) -> p a c", a=16),
                  io["xnT"].rearrange("(a p) c -> p a c", p=128), ldc)
    b.dma("sync", gsc[:], io["gsc_p"], ldc)
    ms_usc = b.dma("sync", usc[:], io["usc_p"], ldc)
    ms_c128 = b.op("vector", "memset", c128[:], -128.0)
    b.wait("tensor", ms_xn)
    b.wait("tensor", ms_c128)
    ps_c_free = None
    for tcol in range(4):
        b.wait("tensor", ps_c_free)
        for ci in range(16):
            b.op("tensor", "matmul", ps_c[:], c128[:],
                 xnT[:, ci * TL + tcol * 512: ci * TL + (tcol + 1) * 512],
                 start=(ci == 0), stop=(ci == 15), sem=(mm if ci == 15 else None))
        ms = b.at(mm)
        b.wait("vector", ms)
        ps_c_free = b.op("vector", "tensor_copy",
                         c1[:, tcol * 512:(tcol + 1) * 512], ps_c[:])
    ms_c1 = b.at(ve)
    b.wait("scalar", ms_usc)
    b.wait("scalar", ms_c1)

    wg_use = [None, None]
    wu_use = [None, None]
    hl_store = [None, None]
    ps_g_free = [None, None]
    ps_u_free = [None, None]
    for sl in range(16):
        b.wait("gpsimd", wg_use[sl % 2])
        ms_wg = b.dma("gpsimd", wgb[sl % 2][:].rearrange("p (a c) -> p a c", a=16),
                      io["gate_wT"][:, sl * 512:(sl + 1) * 512]
                      .rearrange("(a p) c -> p a c", p=128), wlg[sl % 2])
        b.wait("gpsimd", wu_use[sl % 2])
        ms_wu = b.dma("gpsimd", wub[sl % 2][:].rearrange("p (a c) -> p a c", a=16),
                      io["up_wT"][:, sl * 512:(sl + 1) * 512]
                      .rearrange("(a p) c -> p a c", p=128), wlu[sl % 2])
        b.wait("tensor", ms_wg)
        b.wait("tensor", ms_wu)
        for isub in range(4):
            ic = sl * 4 + isub
            hl = hline[ic % 2]
            for tcol in range(4):
                pslot = tcol % 2
                b.wait("tensor", ps_g_free[pslot])
                for ci in range(16):
                    b.op("tensor", "matmul", ps_g[pslot][:],
                         wgb[sl % 2][:, ci * 512 + isub * 128: ci * 512 + (isub + 1) * 128],
                         xnT[:, ci * TL + tcol * 512: ci * TL + (tcol + 1) * 512],
                         start=(ci == 0), stop=(ci == 15), sem=(mm if ci == 15 else None))
                ms_g = b.at(mm)
                b.wait("tensor", ps_u_free[pslot])
                for ci in range(16):
                    b.op("tensor", "matmul", ps_u[pslot][:],
                         wub[sl % 2][:, ci * 512 + isub * 128: ci * 512 + (isub + 1) * 128],
                         xnT[:, ci * TL + tcol * 512: ci * TL + (tcol + 1) * 512],
                         start=(ci == 0), stop=(ci == 15), sem=(mm if ci == 15 else None))
                ms_u = b.at(mm)
                b.wait("vector", ms_g)
                tgs = tg[:, pslot * 512:(pslot + 1) * 512]
                tus = tu[:, pslot * 512:(pslot + 1) * 512]
                sgs = sg[:, pslot * 512:(pslot + 1) * 512]
                ms_tg = b.op("vector", "tensor_tensor", tgs, ps_g[pslot][:],
                             c1[:, tcol * 512:(tcol + 1) * 512], ALU.add)
                ps_g_free[pslot] = ms_tg
                b.wait("scalar", ms_tg)
                ms_sg = b.op("scalar", "activation", sgs, tgs, AF.Silu,
                             scale=gsc[:, ic:ic + 1])
                b.wait("vector", ms_u)
                ms_tu = b.op("vector", "tensor_tensor", tus, ps_u[pslot][:],
                             c1[:, tcol * 512:(tcol + 1) * 512], ALU.add)
                ps_u_free[pslot] = ms_tu
                b.wait("vector", ms_sg)
                if tcol == 0:
                    b.wait("vector", hl_store[ic % 2])
                ms_h = b.op("vector", "scalar_tensor_tensor",
                            hl[:, tcol * 512:(tcol + 1) * 512], tus,
                            usc[:, ic:ic + 1], sgs, ALU.mult, ALU.mult)
            b.wait("sync", ms_h)
            hl_store[ic % 2] = b.dma(
                "sync", io["h_hbm"][ic * 128:(ic + 1) * 128, :], hl[:], sth[ic % 2])
        wg_use[sl % 2] = b.at(mm)
        wu_use[sl % 2] = b.at(mm)
    b.wait("sync", b.at(sth[0]))
    b.wait("sync", b.at(sth[1]))
    b.barrier()
    b.pop_scope()

    b.push_scope()
    y_acc = b.sb("s2y", [128, 16 * 1024], F32)
    rs_sb = b.sb("s2rs", [128, 16], F32)
    rsn = b.sb("s2rsn", [128, 16], F32)
    dsc = b.sb("s2dsc", [128, D], F32)
    ones1 = b.sb("s2one", [128, 1], BF16)
    wdb = [b.sb(f"s2wd{i}", [128, 8 * 1024], BF16) for i in range(2)]
    hb = [b.sb(f"s2hb{i}", [128, 8 * TL], BF16) for i in range(2)]
    ty = b.sb("s2ty", [128, 1024], F32)
    g0f = [b.sb(f"s2g0{i}", [128, 1024], F32) for i in range(2)]
    ps_y = [b.ps(f"s2py{i}", [128, 1024], F32) for i in range(2)]
    ps_rs = b.ps("s2prs", [128, 1], F32)

    b.dma("sync", dsc[:], io["dsc_b"], ldc)
    ms_ones = b.dma("sync", ones1[:], io["ones1_bf"], ldc)
    b.wait("tensor", ms_ones)

    wd_use = [None, None]
    h_use = [None, None]
    g0_store = [None, None]
    ps_y_free = [None, None]
    ps_rs_free = None
    for blk in range(2):
        for mac in range(8):
            b.wait("gpsimd", wd_use[mac % 2])
            ms_wd = b.dma("gpsimd", wdb[mac % 2][:].rearrange("p (a c) -> p a c", a=8),
                          io["down_wT"][mac * 1024:(mac + 1) * 1024,
                                        blk * 1024:(blk + 1) * 1024]
                          .rearrange("(a p) c -> p a c", p=128), wld[mac % 2])
            b.wait("sync", h_use[mac % 2])
            ms_h = b.dma("sync", hb[mac % 2][:].rearrange("p (a c) -> p a c", a=8),
                         io["h_hbm"][mac * 1024:(mac + 1) * 1024, :]
                         .rearrange("(a p) c -> p a c", p=128), ldh[mac % 2])
            b.wait("tensor", ms_wd)
            b.wait("tensor", ms_h)
            for tk in range(16):
                yslot = tk % 2
                b.wait("tensor", ps_y_free[yslot])
                for hc in range(2):
                    for ic in range(8):
                        b.op("tensor", "matmul", ps_y[yslot][:, hc * 512:(hc + 1) * 512],
                             hb[mac % 2][:, ic * TL + tk * 128: ic * TL + (tk + 1) * 128],
                             wdb[mac % 2][:, ic * 1024 + hc * 512: ic * 1024 + (hc + 1) * 512],
                             start=(ic == 0), stop=(ic == 7),
                             sem=(mm if (ic == 7 and hc == 1) else None))
                ms_y = b.at(mm)
                if blk == 0:
                    b.wait("tensor", ps_rs_free)
                    for ic in range(8):
                        b.op("tensor", "matmul", ps_rs[:],
                             hb[mac % 2][:, ic * TL + tk * 128: ic * TL + (tk + 1) * 128],
                             ones1[:], start=(ic == 0), stop=(ic == 7),
                             sem=(mm if ic == 7 else None))
                    ms_rs = b.at(mm)
                b.wait("vector", ms_y)
                if mac == 0:
                    ms_ya = b.op("vector", "tensor_copy",
                                 y_acc[:, tk * 1024:(tk + 1) * 1024], ps_y[yslot][:])
                else:
                    ms_ya = b.op("vector", "tensor_tensor",
                                 y_acc[:, tk * 1024:(tk + 1) * 1024],
                                 y_acc[:, tk * 1024:(tk + 1) * 1024], ps_y[yslot][:], ALU.add)
                ps_y_free[yslot] = ms_ya
                if blk == 0:
                    b.wait("vector", ms_rs)
                    if mac == 0:
                        ms_rse = b.op("vector", "tensor_copy", rs_sb[:, tk:tk + 1], ps_rs[:])
                    else:
                        ms_rse = b.op("vector", "tensor_tensor", rs_sb[:, tk:tk + 1],
                                      rs_sb[:, tk:tk + 1], ps_rs[:], ALU.add)
                    ps_rs_free = ms_rse
            wd_use[mac % 2] = b.at(mm)
            h_use[mac % 2] = b.at(mm)
        if blk == 0:
            ms_rsn = b.op("vector", "tensor_scalar_mul", rsn[:], rs_sb[:], -128.0)
        for tk in range(16):
            ms_t = b.op("vector", "tensor_scalar", ty[:],
                        y_acc[:, tk * 1024:(tk + 1) * 1024], rsn[:, tk:tk + 1],
                        None, ALU.add)
            b.wait("vector", g0_store[tk % 2])
            ms_y2 = b.op("vector", "tensor_tensor", g0f[tk % 2][:], ty[:],
                         dsc[:, blk * 1024:(blk + 1) * 1024], ALU.mult)
            b.wait("scalar", ms_y2)
            ms_g0 = b.op("scalar", "activation", g0f[tk % 2][:], g0f[tk % 2][:], AF.Silu)
            b.wait("sync", ms_g0)
            g0_store[tk % 2] = b.dma(
                "sync", io["g0"][tk * 128:(tk + 1) * 128, blk * 1024:(blk + 1) * 1024],
                g0f[tk % 2][:], stg[tk % 2])
    b.wait("sync", b.at(stg[0]))
    b.wait("sync", b.at(stg[1]))
    b.pop_scope()
    return


def stage_s3(b, io):
    if "ccM" in b._semcache:
        b.wait("sync", (b.sem("ccM"), 1))
    ldc = b.sem("ldc")
    ldq = [b.sem("ldq0"), b.sem("ldq1")]
    ldm = [b.sem("ldm0"), b.sem("ldm1")]
    ldx = [b.sem("ldx0"), b.sem("ldx1")]
    ldg = [b.sem("ldg0"), b.sem("ldg1")]
    ve = b.eng_sem("vector")
    sc = b.eng_sem("scalar")
    mm = b.sem("mm")
    stx = [b.sem("stx0"), b.sem("stx1")]

    o_sb = b.sb("s3o", [128, 16 * TL], BF16)
    wo_sb = b.sb("s3wo", [128, 16 * D], BF16)
    kviw = b.sb("s3kviw", [128, NCORE * H], F32)
    qdecb = b.sb("s3qdec", [128, 16 * 128], BF16)
    blkc = b.sb("s3blk", [128, 16], F32)
    Mb = [b.sb(f"s3M{i}", [128, NCORE * 128], F32) for i in range(2)]
    qtl = [b.sb(f"s3q{i}", [128, SL], BF16) for i in range(2)]
    kvcur = b.sb("s3kvc", [128, 128], F32)
    kvbf = b.sb("s3kvb", [128, 128], BF16)
    qsc = b.sb("s3qsc", [128, 128], BF16)
    xt = [b.sb(f"s3x{i}", [128, D], F32) for i in range(2)]
    gt = [b.sb(f"s3g{i}", [128, D], F32) for i in range(2)]
    tmul = b.sb("s3tm", [128, 512], F32)
    x1t = [b.sb(f"s3x1{i}", [128, D], F32) for i in range(2)]

    ps_corr = b.ps("s3pc", [128, 128], F32)
    ps_s0 = [b.ps(f"s3ps{i}", [128, 512], F32) for i in range(2)]

    ms_o = b.dma("sync", o_sb[:].rearrange("p (a c) -> p a c", a=16),
                 io["o_loc"].rearrange("(a p) c -> p a c", p=128), ldc)
    ms_wo = b.dma("gpsimd", wo_sb[:].rearrange("p (a c) -> p a c", a=16),
                  io["woT"].rearrange("(a p) c -> p a c", p=128), ldc if False else b.sem("wlo"))
    b.dma("sync", kviw[:], io["kvinitw"], ldc)
    b.dma("sync", qdecb[:].rearrange("p (a c) -> p a c", a=16),
          io["qdecb_bf"].rearrange("(a p) c -> p a c", p=128), ldc)
    ms_blk = b.dma("sync", blkc[:], io["blkc"], ldc)
    ms_consts = b.at(ldc)

    b.wait("vector", ms_consts)
    M_use = [None, None]
    q_use = [None, None]
    ps_corr_free = None
    for bb in range(B):
        for h in range(H):
            bh = bb * H + h
            b.wait("sync", M_use[bh % 2])
            src = io["M_all"].rearrange("(r n) c -> r n c", r=NCORE)[
                :, bh * 128:(bh + 1) * 128, :].rearrange("r n c -> n r c")
            ms_M = b.dma("sync", Mb[bh % 2][:].rearrange("p (r c) -> p r c", r=NCORE),
                         src, ldm[bh % 2])
            b.wait("sync", q_use[bh % 2])
            ms_q = b.dma("sync", qtl[bh % 2][:],
                         io["qT"][h * 128:(h + 1) * 128, bb * SL:(bb + 1) * SL],
                         ldq[bh % 2])
            b.wait("vector", ms_M)
            ms = b.op("vector", "tensor_scalar", kvcur[:],
                      Mb[bh % 2][:, 0:128], kviw[:, 0 * H + h:0 * H + h + 1], None,
                      ALU.mult)
            for cp in range(1, NCORE):
                ms = b.op("vector", "scalar_tensor_tensor", kvcur[:],
                          Mb[bh % 2][:, cp * 128:(cp + 1) * 128],
                          kviw[:, cp * H + h:cp * H + h + 1],
                          kvcur[:], ALU.mult, ALU.add)
            M_use[bh % 2] = ms
            b.wait("vector", ms_q)
            for t in range(NCH):
                ms_kvbf = b.op("vector", "tensor_copy", kvbf[:], kvcur[:])
                ms_qsc = b.op("vector", "tensor_tensor", qsc[:],
                              qtl[bh % 2][:, t * 128:(t + 1) * 128],
                              qdecb[:, h * 128:(h + 1) * 128], ALU.mult)
                b.wait("tensor", ms_qsc)
                b.wait("tensor", ps_corr_free)
                ms_cmm = b.op("tensor", "matmul", ps_corr[:], kvbf[:], qsc[:],
                              start=True, stop=True, sem=mm)
                b.wait("vector", ms_cmm)
                oc = o_sb[:, h * TL + bb * SL + t * 128: h * TL + bb * SL + (t + 1) * 128]
                ms_oadd = b.op("vector", "tensor_tensor", oc, oc, ps_corr[:], ALU.add)
                ps_corr_free = ms_oadd
                if t < NCH - 1:
                    b.op("vector", "tensor_scalar", kvcur[:], kvcur[:],
                         blkc[:, h:h + 1], None, ALU.mult)
            q_use[bh % 2] = ms_cmm

    b.wait("tensor", b.at(ve))
    b.wait("tensor", ms_wo)
    x_use = [None, None]
    g_use = [None, None]
    x1_st = [None, None]
    ps_free = [None, None]
    for tk in range(16):
        b.wait("sync", x_use[tk % 2])
        ms_x = b.dma("sync", xt[tk % 2][:], io["x0c"][tk * 128:(tk + 1) * 128, :],
                     ldx[tk % 2])
        b.wait("sync", g_use[tk % 2])
        ms_g = b.dma("sync", gt[tk % 2][:], io["g0"][tk * 128:(tk + 1) * 128, :],
                     ldg[tk % 2])
        for dcol in range(4):
            b.wait("tensor", ps_free[dcol % 2])
            for ci in range(16):
                b.op("tensor", "matmul", ps_s0[dcol % 2][:],
                     o_sb[:, ci * TL + tk * 128: ci * TL + (tk + 1) * 128],
                     wo_sb[:, ci * D + dcol * 512: ci * D + (dcol + 1) * 512],
                     start=(ci == 0), stop=(ci == 15), sem=(mm if ci == 15 else None))
            ms_s0 = b.at(mm)
            b.wait("vector", ms_s0)
            if dcol == 0:
                b.wait("vector", ms_x)
                b.wait("vector", ms_g)
                b.wait("vector", x1_st[tk % 2])
            ms_t = b.op("vector", "tensor_tensor", tmul[:], ps_s0[dcol % 2][:],
                        gt[tk % 2][:, dcol * 512:(dcol + 1) * 512], ALU.mult)
            ps_free[dcol % 2] = ms_t
            ms_x1 = b.op("vector", "tensor_tensor",
                         x1t[tk % 2][:, dcol * 512:(dcol + 1) * 512],
                         xt[tk % 2][:, dcol * 512:(dcol + 1) * 512], tmul[:], ALU.add)
        x_use[tk % 2] = ms_x1
        g_use[tk % 2] = ms_x1
        b.wait("sync", ms_x1)
        x1_st[tk % 2] = b.dma("sync", io["x1c"][tk * 128:(tk + 1) * 128, :],
                              x1t[tk % 2][:], stx[tk % 2])
    b.wait("sync", b.at(stx[0]))
    b.wait("sync", b.at(stx[1]))
    return


def emit_ln(b, pre, x_ap, xnT, ident, epscol):
    ldx = [b.sem("ldx0"), b.sem("ldx1")]
    ve = b.eng_sem("vector")
    sc = b.eng_sem("scalar")
    mm = b.sem("mm")
    NT = TL // 128
    xt = [b.sb(f"{pre}xt{i}", [128, D], F32) for i in range(2)]
    xnt = [b.sb(f"{pre}xnt{i}", [128, D], BF16) for i in range(2)]
    sq = b.sb(f"{pre}sq", [128, D], BF16)
    stat = b.sb(f"{pre}stat", [128, 8], F32)
    ps_tr = [b.ps(f"{pre}tr{i}", [128, 128], BF16) for i in range(2)]

    xn_done = [None] * NT
    evac_done = [None] * NT
    tr_evac = [None, None]
    prev_ms_xn = None
    for T in range(NT):
        if T >= 2:
            b.wait("sync", xn_done[T - 2])
        ms_load = b.dma("sync", xt[T % 2][:], x_ap[T * 128:(T + 1) * 128, :], ldx[T % 2])
        xb = xt[T % 2]
        xnb = xnt[T % 2]
        b.wait("scalar", ms_load)
        if prev_ms_xn is not None:
            b.wait("scalar", prev_ms_xn)
        ms_ssq = b.op("scalar", "activation", sq[:], xb[:], AF.Square,
                      accum_out=stat[:, 2:3])
        b.wait("vector", ms_load)
        b.op("vector", "reduce_sum", stat[:, 0:1], xb[:], axis=AX.X)
        b.op("vector", "tensor_scalar_mul", stat[:, 1:2], stat[:, 0:1], 1.0 / D)
        b.wait("vector", ms_ssq)
        b.op("vector", "tensor_tensor", stat[:, 5:6], stat[:, 1:2], stat[:, 1:2], ALU.mult)
        b.op("vector", "tensor_scalar", stat[:, 6:7], stat[:, 2:3], 1.0 / D, None, ALU.mult)
        ms_var = b.op("vector", "tensor_sub", stat[:, 6:7], stat[:, 6:7], stat[:, 5:6])
        b.wait("scalar", ms_var)
        b.op("scalar", "activation", stat[:, 3:4], stat[:, 6:7], AF.Ln, bias=epscol[:, 0:1])
        ms_rstd = b.op("scalar", "activation", stat[:, 4:5], stat[:, 3:4], AF.Exp,
                       scale=-0.5)
        b.wait("vector", ms_rstd)
        b.op("vector", "tensor_tensor", stat[:, 7:8], stat[:, 1:2], stat[:, 4:5], ALU.mult)
        b.op("vector", "tensor_scalar_mul", stat[:, 7:8], stat[:, 7:8], -1.0)
        if T >= 2:
            b.wait("vector", evac_done[T - 2])
        ms_xn = b.op("vector", "tensor_scalar", xnb[:], xb[:], stat[:, 4:5],
                     stat[:, 7:8], ALU.mult, ALU.add)
        xn_done[T] = ms_xn
        prev_ms_xn = ms_xn
        b.wait("tensor", ms_xn)
        for ci in range(16):
            b.wait("tensor", tr_evac[ci % 2])
            ms_t = b.op("tensor", "transpose", ps_tr[ci % 2][:],
                        xnb[:, ci * 128:(ci + 1) * 128], ident[:], sem=mm)
            b.wait("vector", ms_t)
            tr_evac[ci % 2] = b.op("vector", "tensor_copy",
                                   xnT[:, ci * TL + T * 128: ci * TL + (T + 1) * 128],
                                   ps_tr[ci % 2][:])
        evac_done[T] = b.at(ve)
    return b.at(ve)


def stage_s4(b, io):
    ldc = b.sem("ldc")
    ve = b.eng_sem("vector")
    sc = b.eng_sem("scalar")
    mm = b.sem("mm")
    wl = [b.sem("wl0"), b.sem("wl1")]
    stk = [b.sem("stk0"), b.sem("stk1")]
    stv = [b.sem("stv0"), b.sem("stv1")]

    wvo_sb = b.sb("s4wvo", [128, 16 * D], BF16)
    ident = b.sb("s4id", [128, 128], BF16)
    epscol = b.sb("s4eps", [128, 1], F32)
    wbuf = [b.sb(f"s4wb{i}", [128, 16 * 128], BF16) for i in range(2)]
    kst = [b.sb(f"s4ks{i}", [128, 512], BF16) for i in range(2)]
    vst = [b.sb(f"s4vs{i}", [128, 512], BF16) for i in range(2)]
    ps_k = [b.ps(f"s4pk{i}", [128, 512], F32) for i in range(2)]
    ps_v = [b.ps(f"s4pv{i}", [128, 512], F32) for i in range(2)]

    ms_id = b.dma("sync", ident[:], io["ident_bf"], ldc)
    b.op("vector", "memset", epscol[:], EPS)
    b.wait("tensor", ms_id)

    b.push_scope()
    wo_full = b.sb("s4wof", [128, 16 * D], BF16)
    ms_wof = b.dma("gpsimd", wo_full[:].rearrange("p (a e) -> p a e", a=16),
                   io["Wo1"].rearrange("(a p) e -> p a e", p=128), b.sem("wlof"))
    wvt_b2 = [b.sb(f"s4wvtb{i}", [128, 16 * 128], BF16) for i in range(2)]
    wvt_use = [None, None]
    psv_free = [None, None]
    b.wait("tensor", ms_wof)
    for dch in range(16):
        b.wait("gpsimd", wvt_use[dch % 2])
        ms_wvt = b.dma("gpsimd", wvt_b2[dch % 2][:].rearrange("p (a e) -> p a e", a=16),
                       io["WvT"][:, dch * 128:(dch + 1) * 128]
                       .rearrange("(a p) e -> p a e", p=128), wl[dch % 2])
        b.wait("tensor", ms_wvt)
        for ec in range(4):
            sl2 = ec % 2
            b.wait("tensor", psv_free[sl2])
            for mc in range(16):
                b.op("tensor", "matmul", ps_v[sl2][:],
                     wvt_b2[dch % 2][:, mc * 128:(mc + 1) * 128],
                     wo_full[:, mc * D + ec * 512:(mc) * D + ec * 512 + 512],
                     start=(mc == 0), stop=(mc == 15), sem=(mm if mc == 15 else None))
            ms_mm = b.at(mm)
            b.wait("vector", ms_mm)
            psv_free[sl2] = b.op("vector", "tensor_copy",
                                 wvo_sb[:, dch * D + ec * 512: dch * D + (ec + 1) * 512],
                                 ps_v[sl2][:])
        wvt_use[dch % 2] = b.at(mm)
    b.wait("tensor", b.at(ve))
    b.pop_scope()

    xnT = b.sb("s4xnT", [128, 16 * TL], BF16)
    emit_ln(b, "s4", io["x1c"], xnT, ident, epscol)

    wuse = [None, None]
    kfree = [None, None]
    b.wait("tensor", b.at(ve))
    for h in range(H):
        b.wait("gpsimd", wuse[h % 2])
        ms_w = b.dma("gpsimd", wbuf[h % 2][:].rearrange("p (a e) -> p a e", a=16),
                     io["Wk1"][:, h * 128:(h + 1) * 128]
                     .rearrange("(a p) e -> p a e", p=128), wl[h % 2])
        b.wait("tensor", ms_w)
        for tc in range(4):
            bb = tc // 2
            pcol = (tc % 2) * 512
            b.wait("tensor", kfree[tc % 2])
            for ci in range(16):
                b.op("tensor", "matmul", ps_k[tc % 2][:],
                     wbuf[h % 2][:, ci * 128:(ci + 1) * 128],
                     xnT[:, ci * TL + tc * 512: ci * TL + (tc + 1) * 512],
                     start=(ci == 0), stop=(ci == 15), sem=(mm if ci == 15 else None))
            ms_mm = b.at(mm)
            b.wait("vector", ms_mm)
            b.wait("vector", kfree[tc % 2])
            ms_ev = b.op("vector", "tensor_copy", kst[tc % 2][:], ps_k[tc % 2][:])
            b.wait("sync", ms_ev)
            ms_st = b.dma("sync",
                          io["Kag"][(bb * H + h) * 128:(bb * H + h + 1) * 128,
                                    pcol:pcol + 512],
                          kst[tc % 2][:], stk[tc % 2])
            kfree[tc % 2] = ms_st
        wuse[h % 2] = b.at(mm)

    vfree = [None, None]
    for ec in range(4):
        for tk in range(16):
            sl2 = tk % 2
            b.wait("tensor", vfree[sl2])
            for ci in range(16):
                b.op("tensor", "matmul", ps_v[sl2][:],
                     xnT[:, ci * TL + tk * 128: ci * TL + (tk + 1) * 128],
                     wvo_sb[:, ci * D + ec * 512: ci * D + (ec + 1) * 512],
                     start=(ci == 0), stop=(ci == 15), sem=(mm if ci == 15 else None))
            ms_v = b.at(mm)
            b.wait("vector", ms_v)
            b.wait("vector", vfree[sl2])
            ms_ev = b.op("vector", "tensor_copy", vst[sl2][:], ps_v[sl2][:])
            b.wait("sync", ms_ev)
            vfree[sl2] = b.dma("sync",
                               io["Vag"][tk * 128:(tk + 1) * 128, ec * 512:(ec + 1) * 512],
                               vst[sl2][:], stv[sl2])
    for sem in (stk[0], stk[1], stv[0], stv[1]):
        b.wait("sync", b.at(sem))
    return


def stage_s5(b, io):
    ldc = b.sem("ldc")
    ve = b.eng_sem("vector")
    sc = b.eng_sem("scalar")
    mm = b.sem("mm")
    wl = [b.sem("wl0"), b.sem("wl1")]
    wl2 = [b.sem("wl20"), b.sem("wl21")]
    ldh = [b.sem("ldh0"), b.sem("ldh1")]
    stq = [b.sem("stq0"), b.sem("stq1")]
    sth = [b.sem("sth0"), b.sem("sth1")]
    stg = [b.sem("stg0"), b.sem("stg1")]

    b.push_scope()
    xnT = b.sb("s5xnT", [128, 16 * TL], BF16)
    ident = b.sb("s5id", [128, 128], BF16)
    epscol = b.sb("s5eps", [128, 1], F32)
    ms_id = b.dma("sync", ident[:], io["ident_bf"], ldc)
    b.op("vector", "memset", epscol[:], EPS)
    b.wait("tensor", ms_id)
    emit_ln(b, "s5", io["x1s"], xnT, ident, epscol)

    wbuf = [b.sb(f"s5wb{i}", [128, 16 * 128], BF16) for i in range(2)]
    qst = [b.sb(f"s5qs{i}", [128, 512], BF16) for i in range(2)]
    ps_q = [b.ps(f"s5pq{i}", [128, 512], F32) for i in range(2)]
    wuse = [None, None]
    qfree = [None, None]
    b.wait("tensor", b.at(ve))
    for h in range(H):
        b.wait("gpsimd", wuse[h % 2])
        ms_w = b.dma("gpsimd", wbuf[h % 2][:].rearrange("p (a e) -> p a e", a=16),
                     io["Wq1"][:, h * 128:(h + 1) * 128]
                     .rearrange("(a p) e -> p a e", p=128), wl[h % 2])
        b.wait("tensor", ms_w)
        for tc in range(4):
            b.wait("tensor", qfree[tc % 2])
            for ci in range(16):
                b.op("tensor", "matmul", ps_q[tc % 2][:],
                     wbuf[h % 2][:, ci * 128:(ci + 1) * 128],
                     xnT[:, ci * TL + tc * 512: ci * TL + (tc + 1) * 512],
                     start=(ci == 0), stop=(ci == 15), sem=(mm if ci == 15 else None))
            ms_mm = b.at(mm)
            b.wait("vector", ms_mm)
            b.wait("vector", qfree[tc % 2])
            ms_ev = b.op("vector", "tensor_copy", qst[tc % 2][:], ps_q[tc % 2][:])
            b.wait("sync", ms_ev)
            qfree[tc % 2] = b.dma("sync",
                                  io["Q1T"][h * 128:(h + 1) * 128,
                                            tc * 512:(tc + 1) * 512],
                                  qst[tc % 2][:], stq[tc % 2])
        wuse[h % 2] = b.at(mm)

    wub = [b.sb(f"s5wu{i}", [128, 16 * 512], BF16) for i in range(2)]
    usc = b.sb("s5usc", [128, 64], F32)
    hline = [b.sb(f"s5hl{i}", [128, TL], BF16) for i in range(2)]
    ps_u = [b.ps(f"s5pu{i}", [128, 512], F32) for i in range(2)]
    ms_usc = b.dma("sync", usc[:], io["usc1_p"], ldc)
    b.wait("scalar", ms_usc)
    wu_use = [None, None]
    hl_store = [None, None]
    ps_u_free = [None, None]
    for sl in range(16):
        b.wait("gpsimd", wu_use[sl % 2])
        ms_wu = b.dma("gpsimd", wub[sl % 2][:].rearrange("p (a c) -> p a c", a=16),
                      io["up_wT1"][:, sl * 512:(sl + 1) * 512]
                      .rearrange("(a p) c -> p a c", p=128), wl2[sl % 2])
        b.wait("scalar", ms_wu)
        ms_sgn = b.op("scalar", "activation", wub[sl % 2][:], wub[sl % 2][:], AF.Sign)
        b.wait("tensor", ms_sgn)
        for isub in range(4):
            ic = sl * 4 + isub
            hl = hline[ic % 2]
            for tcol in range(4):
                pslot = tcol % 2
                b.wait("tensor", ps_u_free[pslot])
                for ci in range(16):
                    b.op("tensor", "matmul", ps_u[pslot][:],
                         wub[sl % 2][:, ci * 512 + isub * 128: ci * 512 + (isub + 1) * 128],
                         xnT[:, ci * TL + tcol * 512: ci * TL + (tcol + 1) * 512],
                         start=(ci == 0), stop=(ci == 15), sem=(mm if ci == 15 else None))
                ms_u = b.at(mm)
                b.wait("scalar", ms_u)
                if tcol == 0:
                    b.wait("scalar", hl_store[ic % 2])
                ms_h = b.op("scalar", "activation", hl[:, tcol * 512:(tcol + 1) * 512],
                            ps_u[pslot][:], AF.Relu, scale=usc[:, ic:ic + 1])
                ps_u_free[pslot] = ms_h
            b.wait("sync", ms_h)
            hl_store[ic % 2] = b.dma(
                "sync", io["h1_hbm"][ic * 128:(ic + 1) * 128, :], hl[:], sth[ic % 2])
        wu_use[sl % 2] = b.at(mm)
    b.wait("sync", b.at(sth[0]))
    b.wait("sync", b.at(sth[1]))
    b.barrier()
    b.pop_scope()

    b.push_scope()
    y_acc = b.sb("s5y", [128, 16 * 1024], F32)
    dsc = b.sb("s5dsc", [128, D], F32)
    wdb = [b.sb(f"s5wd{i}", [128, 8 * 1024], BF16) for i in range(2)]
    hb = [b.sb(f"s5hb{i}", [128, 8 * TL], BF16) for i in range(2)]
    ty = b.sb("s5ty", [128, 1024], F32)
    g1f = [b.sb(f"s5g1{i}", [128, 1024], F32) for i in range(2)]
    osc = b.sb("s5osc", [128, 1], F32)
    ps_y = [b.ps(f"s5py{i}", [128, 1024], F32) for i in range(2)]

    b.dma("sync", dsc[:], io["dsc1_b"], ldc)
    ms_osc = b.dma("sync", osc[:], io["osc_c"], ldc)
    b.wait("vector", ms_osc)
    ms_dsc = b.op("vector", "tensor_scalar", dsc[:], dsc[:], osc[:, 0:1], None, ALU.mult)

    wd_use = [None, None]
    h_use = [None, None]
    g1_store = [None, None]
    ps_y_free = [None, None]
    for blk in range(2):
        for mac in range(8):
            b.wait("gpsimd", wd_use[mac % 2])
            ms_wd = b.dma("gpsimd", wdb[mac % 2][:].rearrange("p (a c) -> p a c", a=8),
                          io["down_wT1"][mac * 1024:(mac + 1) * 1024,
                                         blk * 1024:(blk + 1) * 1024]
                          .rearrange("(a p) c -> p a c", p=128), wl2[mac % 2])
            b.wait("scalar", ms_wd)
            ms_sgn = b.op("scalar", "activation", wdb[mac % 2][:], wdb[mac % 2][:],
                          AF.Sign)
            b.wait("sync", h_use[mac % 2])
            ms_h = b.dma("sync", hb[mac % 2][:].rearrange("p (a c) -> p a c", a=8),
                         io["h1_hbm"][mac * 1024:(mac + 1) * 1024, :]
                         .rearrange("(a p) c -> p a c", p=128), ldh[mac % 2])
            b.wait("tensor", ms_sgn)
            b.wait("tensor", ms_h)
            for tk in range(16):
                yslot = tk % 2
                b.wait("tensor", ps_y_free[yslot])
                for hc in range(2):
                    for ic in range(8):
                        b.op("tensor", "matmul", ps_y[yslot][:, hc * 512:(hc + 1) * 512],
                             hb[mac % 2][:, ic * TL + tk * 128: ic * TL + (tk + 1) * 128],
                             wdb[mac % 2][:, ic * 1024 + hc * 512: ic * 1024 + (hc + 1) * 512],
                             start=(ic == 0), stop=(ic == 7),
                             sem=(mm if (ic == 7 and hc == 1) else None))
                ms_y = b.at(mm)
                b.wait("vector", ms_y)
                if mac == 0:
                    ms_ya = b.op("vector", "tensor_copy",
                                 y_acc[:, tk * 1024:(tk + 1) * 1024], ps_y[yslot][:])
                else:
                    ms_ya = b.op("vector", "tensor_tensor",
                                 y_acc[:, tk * 1024:(tk + 1) * 1024],
                                 y_acc[:, tk * 1024:(tk + 1) * 1024], ps_y[yslot][:], ALU.add)
                ps_y_free[yslot] = ms_ya
            wd_use[mac % 2] = b.at(mm)
            h_use[mac % 2] = b.at(mm)
        for tk in range(16):
            b.wait("vector", g1_store[tk % 2])
            ms_y2 = b.op("vector", "tensor_tensor", g1f[tk % 2][:],
                         y_acc[:, tk * 1024:(tk + 1) * 1024],
                         dsc[:, blk * 1024:(blk + 1) * 1024], ALU.mult)
            b.wait("scalar", ms_y2)
            ms_g1 = b.op("scalar", "activation", g1f[tk % 2][:], g1f[tk % 2][:], AF.Silu)
            b.wait("sync", ms_g1)
            g1_store[tk % 2] = b.dma(
                "sync", io["g1"][tk * 128:(tk + 1) * 128, blk * 1024:(blk + 1) * 1024],
                g1f[tk % 2][:], stg[tk % 2])
    b.wait("sync", b.at(stg[0]))
    b.wait("sync", b.at(stg[1]))
    b.wait("sync", b.at(stq[0]))
    b.wait("sync", b.at(stq[1]))
    b.pop_scope()
    return


def stage_s6(b, io):
    if "ccK" in b._semcache:
        for eng in ("sync", "gpsimd", "tensor"):
            b.wait(eng, (b.sem("ccK"), 1))
            b.wait(eng, (b.sem("ccV"), 1))
    ldc = b.sem("ldc")
    ldk = [b.sem("ldk0"), b.sem("ldk1")]
    ldv = b.sem("ldv0")
    ldq2 = b.sem("ldq2")
    ldw = b.sem("ldw0")
    ldxg = b.sem("ldxg")
    ve = b.eng_sem("vector")
    sc = b.eng_sem("scalar")
    mm = b.sem("mm")
    sto = [b.sem("sto0"), b.sem("sto1")]

    maskc = b.sb("s6mask", [128, 1024], F32)
    ident = b.sb("s6id", [128, 128], BF16)
    qg = b.sb("s6qg", [128, 16 * 512], BF16)
    ka = [b.sb(f"s6ka{i}", [128, 16 * 1024], BF16) for i in range(2)]
    junk = b.sb("s6junk", [128, 4 * 1024], BF16)
    pline = b.sb("s6pl", [128, 1024], BF16)
    PT = b.sb("s6PT", [128, 8 * 512], BF16)
    vb = b.sb("s6vb", [128, 4 * D], BF16)
    nm = b.sb("s6nm", [128, 64], F32)
    dd = b.sb("s6dd", [128, 64], F32)
    bias = b.sb("s6bias", [128, 64], F32)
    sm1 = b.sb("s6sm1", [128, 8], F32)
    cf = b.sb("s6cf", [128, 2], F32)
    xg1 = b.sb("s6x1", [128, D], F32)
    gg1 = b.sb("s6g1", [128, D], F32)
    outt = [b.sb(f"s6out{i}", [128, D], F32) for i in range(2)]


    ms_mask = b.dma("sync", maskc[:], io["l1mask"], ldc)
    ms_id = b.dma("sync", ident[:], io["ident_bf"], ldc)
    b.wait("vector", ms_mask)
    b.wait("tensor", ms_id)

    ka_use = [None, None]
    v_use = None
    out_st = [None, None]
    pv_done = None
    wo_use = None
    for bb in range(B):
        for gg2 in range(2):
            blocks = [4 * gg2 + jj for jj in range(4)]
            jmax = blocks[-1]
            b.wait("sync", wo_use)
            ms_qg = b.dma("sync", qg[:].rearrange("p (h c) -> p h c", h=16),
                          io["Q1T"][:, bb * SL + gg2 * 512: bb * SL + (gg2 + 1) * 512]
                          .rearrange("(h p) c -> p h c", p=128), ldq2)
            b.wait("tensor", ms_qg)
            b.cur_stage = f"s6pa{bb}{gg2}"
            b.op("vector", "memset", nm[:], BIG)
            b.op("vector", "memset", dd[:], 0.0)
            b.push_scope()
            ps_sa = [b.ps(f"s6psa{bb}{gg2}{i}", [128, 1024], F32) for i in range(2)]
            ps_free = [None, None]
            pend_d = None
            it = 0
            for u in range(jmax + 1):
                b.wait("gpsimd", ka_use[u % 2])
                ms_k = b.dma("gpsimd", ka[u % 2][:].rearrange("p (h c) -> p h c", h=16),
                             io["Kall"].rearrange("(r n) c -> r n c", r=NCORE)
                             [u, bb * 2048:(bb + 1) * 2048, :]
                             .rearrange("(h p) c -> p h c", p=128), ldk[u % 2])
                b.wait("tensor", ms_k)
                for jj, j in enumerate(blocks):
                    if u > j:
                        continue
                    diag = (u == j)
                    for h in range(H):
                        col = jj * 16 + h
                        sl2 = it % 2
                        it += 1
                        sb_ = sl2 * 4
                        b.wait("tensor", ps_free[sl2])
                        for half in range(2):
                            b.op("tensor", "matmul",
                                 ps_sa[sl2][:, half * 512:(half + 1) * 512],
                                 qg[:, h * 512 + jj * 128: h * 512 + (jj + 1) * 128],
                                 ka[u % 2][:, h * 1024 + half * 512: h * 1024 + (half + 1) * 512],
                                 start=True, stop=True,
                                 sem=(mm if half == 1 else None))
                        ms_s = b.at(mm)
                        b.wait("vector", ms_s)
                        if diag:
                            b.op("vector", "tensor_tensor", ps_sa[sl2][:], ps_sa[sl2][:],
                                 maskc[:], ALU.add)
                        b.op("vector", "reduce_max", sm1[:, sb_:sb_ + 1], ps_sa[sl2][:],
                             axis=AX.X, negate=True)
                        if u == 0:
                            ms_nm = b.op("vector", "tensor_scalar",
                                         nm[:, col:col + 1], sm1[:, sb_:sb_ + 1],
                                         ASCALE, None, ALU.mult)
                        else:
                            b.op("vector", "tensor_scalar", sm1[:, sb_ + 1:sb_ + 2],
                                 sm1[:, sb_:sb_ + 1], ASCALE, nm[:, col:col + 1],
                                 ALU.mult, ALU.subtract)
                            b.op("vector", "tensor_scalar_min", sm1[:, sb_ + 2:sb_ + 3],
                                 sm1[:, sb_ + 1:sb_ + 2], 0.0)
                            ms_nm = b.op("vector", "tensor_add", nm[:, col:col + 1],
                                         nm[:, col:col + 1], sm1[:, sb_ + 2:sb_ + 3])
                        b.wait("scalar", ms_nm)
                        if u > 0:
                            b.op("scalar", "activation", cf[:, sl2:sl2 + 1],
                                 sm1[:, sb_ + 2:sb_ + 3], AF.Exp)
                        ms_exp = b.op("scalar", "activation",
                                      junk[:, sl2 * 1024:(sl2 + 1) * 1024],
                                      ps_sa[sl2][:], AF.Exp, scale=ASCALE,
                                      bias=nm[:, col:col + 1],
                                      accum_out=sm1[:, sb_ + 3:sb_ + 4])
                        ps_free[sl2] = ms_exp
                        if pend_d is not None:
                            pend_d()
                        def _upd(u=u, col=col, sl2=sl2, sb_=sb_, ms_exp=ms_exp):
                            b.wait("vector", ms_exp)
                            if u == 0:
                                b.op("vector", "tensor_copy", dd[:, col:col + 1],
                                     sm1[:, sb_ + 3:sb_ + 4])
                            else:
                                b.op("vector", "scalar_tensor_tensor",
                                     dd[:, col:col + 1], dd[:, col:col + 1],
                                     cf[:, sl2:sl2 + 1], sm1[:, sb_ + 3:sb_ + 4],
                                     ALU.mult, ALU.add)
                        pend_d = _upd
                ka_use[u % 2] = b.at(mm)
            if pend_d is not None:
                pend_d()
                pend_d = None
            b.wait("tensor", b.at(sc))
            b.pop_scope()
            b.wait("scalar", b.at(ve))
            ms_ln = b.op("scalar", "activation", bias[:], dd[:], AF.Ln)
            b.wait("vector", ms_ln)
            ms_bias = b.op("vector", "tensor_sub", bias[:], nm[:], bias[:])
            b.cur_stage = f"s6pb{bb}{gg2}"
            b.push_scope()
            o2T = b.sb(f"s6o2T{bb}{gg2}", [128, 16 * 512], F32)
            ps_sa = [b.ps(f"s6psb{bb}{gg2}{i}", [128, 1024], F32) for i in range(2)]
            ps_tr = [b.ps(f"s6ptr{bb}{gg2}{i}", [128, 128], BF16) for i in range(2)]
            ps_pv = [b.ps(f"s6ppv{bb}{gg2}{i}", [128, 512], F32) for i in range(2)]
            b.op("vector", "memset", o2T[:], 0.0)
            tr_free = [None, None]
            ps_pv_free = [None, None]
            ps_free = [None, None]
            junk_free = [None, None, None, None]
            pline_free = None
            it = 0
            for u in range(jmax + 1):
                b.wait("gpsimd", ka_use[u % 2])
                ms_k = b.dma("gpsimd", ka[u % 2][:].rearrange("p (h c) -> p h c", h=16),
                             io["Kall"].rearrange("(r n) c -> r n c", r=NCORE)
                             [u, bb * 2048:(bb + 1) * 2048, :]
                             .rearrange("(h p) c -> p h c", p=128), ldk[u % 2])
                b.wait("tensor", ms_k)
                b.wait("vector", pv_done)
                pt_written = None
                for jj, j in enumerate(blocks):
                    if u > j:
                        ms_z = b.op("vector", "memset",
                                    PT[:].rearrange("p (a c) -> p a c", a=8)
                                    [:, :, jj * 128:(jj + 1) * 128], 0.0)
                        pt_written = ms_z
                        continue
                    diag = (u == j)
                    for h in range(H):
                        col = jj * 16 + h
                        sl2 = it % 2
                        it += 1
                        b.wait("tensor", ps_free[sl2])
                        for half in range(2):
                            b.op("tensor", "matmul",
                                 ps_sa[sl2][:, half * 512:(half + 1) * 512],
                                 qg[:, h * 512 + jj * 128: h * 512 + (jj + 1) * 128],
                                 ka[u % 2][:, h * 1024 + half * 512: h * 1024 + (half + 1) * 512],
                                 start=True, stop=True,
                                 sem=(mm if half == 1 else None))
                        ms_s = b.at(mm)
                        if diag:
                            b.wait("vector", ms_s)
                            b.op("vector", "tensor_tensor", ps_sa[sl2][:], ps_sa[sl2][:],
                                 maskc[:], ALU.add)
                            b.wait("scalar", b.at(ve))
                        b.wait("scalar", ms_s)
                        if h == 0:
                            dst = pline[:]
                            b.wait("scalar", pline_free)
                        else:
                            jsl = h % 4
                            dst = junk[:, jsl * 1024:(jsl + 1) * 1024]
                            b.wait("scalar", junk_free[jsl])
                        ms_exp = b.op("scalar", "activation", dst, ps_sa[sl2][:],
                                      AF.Exp, scale=ASCALE, bias=bias[:, col:col + 1])
                        ps_free[sl2] = ms_exp
                        if h > 0:
                            b.wait("vector", ms_exp)
                            junk_free[jsl] = b.op("vector", "tensor_tensor", pline[:],
                                                  pline[:],
                                                  junk[:, jsl * 1024:(jsl + 1) * 1024],
                                                  ALU.add)
                    ms_pl = junk_free[jsl] if H > 1 else b.at(ve)
                    b.wait("tensor", ms_pl)
                    for kt in range(8):
                        b.wait("tensor", tr_free[kt % 2])
                        ms_tr = b.op("tensor", "transpose", ps_tr[kt % 2][:],
                                     pline[:, kt * 128:(kt + 1) * 128],
                                     ident[:], sem=mm)
                        b.wait("vector", ms_tr)
                        tr_free[kt % 2] = b.op(
                            "vector", "tensor_copy",
                            PT[:, kt * 512 + jj * 128: kt * 512 + (jj + 1) * 128],
                            ps_tr[kt % 2][:])
                    pline_free = b.at(mm)
                    pt_written = b.at(ve)
                b.wait("tensor", pt_written)
                for hu in range(2):
                    b.wait("sync", v_use)
                    ms_v = b.dma("sync", vb[:].rearrange("p (a c) -> p a c", a=4),
                                 io["Vall"][u * TL + bb * SL + hu * 512:
                                            u * TL + bb * SL + (hu + 1) * 512, :]
                                 .rearrange("(a p) c -> p a c", p=128), ldv)
                    b.wait("tensor", ms_v)
                    for dch in range(16):
                        sl2 = dch % 2
                        b.wait("tensor", ps_pv_free[sl2])
                        for kt in range(4):
                            b.op("tensor", "matmul", ps_pv[sl2][:],
                                 vb[:, kt * D + dch * 128: kt * D + (dch + 1) * 128],
                                 PT[:, (hu * 4 + kt) * 512:(hu * 4 + kt + 1) * 512],
                                 start=(kt == 0), stop=(kt == 3),
                                 sem=(mm if kt == 3 else None))
                        ms_pv = b.at(mm)
                        b.wait("vector", ms_pv)
                        ms_acc = b.op("vector", "tensor_tensor",
                                      o2T[:, dch * 512:(dch + 1) * 512],
                                      o2T[:, dch * 512:(dch + 1) * 512],
                                      ps_pv[sl2][:], ALU.add)
                        ps_pv_free[sl2] = ms_acc
                    v_use = b.at(mm)
                pv_done = b.at(ve)
                ka_use[u % 2] = b.at(mm)
            b.cur_stage = f"s6wo{bb}{gg2}"
            rowbase = bb * SL + gg2 * 512
            b.push_scope()
            outt_q = [b.sb(f"s6oq{bb}{gg2}0", [128, D], F32)] * 2
            prev_out_store = None
            g1q = b.sb(f"s6gq{bb}{gg2}", [128, D], F32)
            s1bf = b.sb(f"s6s1b{bb}{gg2}", [128, 2 * 128], BF16)
            tmpc = b.sb(f"s6tc{bb}{gg2}", [128, 2 * 128], F32)
            tr_free2 = [None, None]
            for qs in range(4):
                rows = slice(rowbase + qs * 128, rowbase + (qs + 1) * 128)
                b.wait("sync", out_st[0])
                b.wait("sync", out_st[1])
                b.wait("sync", prev_out_store)
                ms_x = b.dma("sync", outt_q[qs % 2][:], io["x1s"][rows, :], ldxg)
                ms_g = b.dma("sync", g1q[:], io["g1"][rows, :],
                             b.sem("ldg2"))
                b.wait("vector", ms_x)
                b.wait("vector", ms_g)
                for dch in range(16):
                    csl = dch % 2
                    ms_cast = b.op("vector", "tensor_copy",
                                   s1bf[:, csl * 128:(csl + 1) * 128],
                                   o2T[:, dch * 512 + qs * 128: dch * 512 + (qs + 1) * 128])
                    tsl = dch % 2
                    b.wait("tensor", ms_cast)
                    b.wait("tensor", tr_free2[tsl])
                    ms_tr = b.op("tensor", "transpose", ps_tr[tsl][:],
                                 s1bf[:, csl * 128:(csl + 1) * 128],
                                 ident[:], sem=mm)
                    b.wait("vector", ms_tr)
                    b.op("vector", "tensor_tensor", tmpc[:, tsl * 128:(tsl + 1) * 128],
                         ps_tr[tsl][:], g1q[:, dch * 128:(dch + 1) * 128], ALU.mult)
                    tr_free2[tsl] = b.op(
                        "vector", "tensor_add",
                        outt_q[qs % 2][:, dch * 128:(dch + 1) * 128],
                        outt_q[qs % 2][:, dch * 128:(dch + 1) * 128],
                        tmpc[:, tsl * 128:(tsl + 1) * 128])
                ms_comb = b.at(ve)
                b.wait("sync", ms_comb)
                out_st[qs % 2] = b.dma("sync", io["outc"][rows, :],
                                       outt_q[qs % 2][:], sto[qs % 2])
                prev_out_store = out_st[qs % 2]
            wo_use = b.at(mm)
            b.wait("sync", b.at(sto[0]))
            b.wait("sync", b.at(sto[1]))
            b.pop_scope()
            b.wait("tensor", b.at(ve))
            b.pop_scope()
    b.wait("sync", b.at(sto[0]))
    b.wait("sync", b.at(sto[1]))
    return


def stage_cc_m(b, io):
    cc = b.sem("ccM")
    b.op("gpsimd", "collective_compute", "AllGather", ALU.bypass,
         replica_groups=[list(range(NCORE))],
         ins=[io["M_core"]], outs=[io["M_all"]], sem=cc)


def stage_cc_x1(b, io):
    ccx = b.sem("ccX")
    b.op("gpsimd", "collective_compute", "AllGather", ALU.bypass,
         replica_groups=[list(range(NCORE))],
         ins=[io["x1c"]], outs=[io["x1all"]], sem=ccx)


def stage_cc_kv(b, io):
    cck = b.sem("ccK")
    ccv = b.sem("ccV")
    b.op("gpsimd", "collective_compute", "AllGather", ALU.bypass,
         replica_groups=[list(range(NCORE))],
         ins=[io["Kag"]], outs=[io["Kall"]], sem=cck)
    b.op("gpsimd", "collective_compute", "AllGather", ALU.bypass,
         replica_groups=[list(range(NCORE))],
         ins=[io["Vag"]], outs=[io["Vall"]], sem=ccv)


def stage_extract_x1s(b, io):
    ldc = b.sem("ldc")
    lde = b.sem("lde")
    sb_off = b.sb("xoff", [1, 1], I32)
    ms_off = b.dma("sync", sb_off[:], io["coff"], ldc)
    b.wait("gpsimd", ms_off)
    start = b.count[lde]
    b.count[lde] += 16 * (NCORE * B)
    x1all = io["x1all"]
    x1s = io["x1s"]

    ccx = b.sem("ccX")

    def fn(g, sb_off=sb_off, start=start, x1all=x1all, x1s=x1s, ccx=ccx):
        g.wait_ge(ccx, 1)
        off = g.value_load(sb_off[0:1, 0:1])
        cnt = start
        for r in range(NCORE):
            for bb in range(B):
                src = x1all.rearrange("(r b m s) d -> r b m s d",
                                      b=B, m=128, s=8)[r, bb][:, bass.ds(off, 1), :]
                dst = x1s[bb * SL + r * 128: bb * SL + (r + 1) * 128, :]
                if cnt > 0:
                    g.wait_ge(lde, cnt)
                g.dma_start(out=dst, in_=src).then_inc(lde, 16)
                cnt += 16
        g.wait_ge(lde, cnt)
    b.emit("gpsimd", fn)


def run_launch(stages, tensors, in_maps_np, n_cores=NCORE):
    nc = bass.Bass("TRN2", target_bir_lowering=False, debug=False)
    io = {}
    out_names = []
    for name, spec in tensors.items():
        shape, dtype, kind = spec[:3]
        aspace = spec[3] if len(spec) > 3 else "Local"
        t = nc.dram_tensor(name, list(shape), dtype, kind=kind, addr_space=aspace)
        io[name] = t.ap()
        if kind == "ExternalOutput":
            out_names.append(name)
    es = ExitStack()
    with es:
        b = Builder(nc, es)
        cc_names = ("ccM", "ccX", "ccK", "ccV")
        for st in stages:
            b.cur_stage = st.__name__
            b.push_scope()
            st(b, io)
            b.barrier(skip=cc_names)
            b.pop_scope()
        b.finalize()
    res = run_bass_kernel_spmd(nc, in_maps_np, core_ids=list(range(n_cores)))
    return [{n: r[n] for n in out_names} for r in res.results]


def _prep_weights(inputs):
    w = {}
    w["wqT"] = np.ascontiguousarray(np.asarray(inputs["L0_attn_wq"]).T).astype(np.float32)
    w["wkT"] = np.ascontiguousarray(np.asarray(inputs["L0_attn_wk"]).T).astype(np.float32)
    w["wvT"] = np.ascontiguousarray(np.asarray(inputs["L0_attn_wv"]).T).astype(np.float32)
    w["woT"] = np.ascontiguousarray(np.asarray(inputs["L0_attn_wo"]).T).astype(np.float32)
    w["gate_wT"] = np.ascontiguousarray(np.asarray(inputs["L0_ffn_gate_w"]).T).astype(np.int32)
    w["up_wT"] = np.ascontiguousarray(np.asarray(inputs["L0_ffn_up_w"]).T).astype(np.int32)
    w["down_wT"] = np.ascontiguousarray(np.asarray(inputs["L0_ffn_down_w"]).T).astype(np.int32)
    w["gsc_p"] = np.ascontiguousarray(
        np.asarray(inputs["L0_ffn_gate_scale"]).astype(np.float32).reshape(64, 128).T)
    w["usc_p"] = np.ascontiguousarray(
        np.asarray(inputs["L0_ffn_up_scale"]).astype(np.float32).reshape(64, 128).T)
    w["dsc_b"] = np.broadcast_to(
        np.asarray(inputs["L0_ffn_down_scale"]).astype(np.float32), (128, D)).copy()
    w["Wq1"] = np.ascontiguousarray(np.asarray(inputs["L1_attn_W_q"]).astype(np.float32))
    w["Wk1"] = np.ascontiguousarray(np.asarray(inputs["L1_attn_W_k"]).astype(np.float32))
    w["Wv1"] = np.ascontiguousarray(np.asarray(inputs["L1_attn_W_v"]).astype(np.float32))
    w["WvT"] = np.ascontiguousarray(np.asarray(inputs["L1_attn_W_v"]).T).astype(np.float32)
    w["Wo1"] = np.ascontiguousarray(np.asarray(inputs["L1_attn_W_o"]).astype(np.float32))
    w["up_wT1"] = np.ascontiguousarray(np.asarray(inputs["L1_ffn_up_weight"]).T).astype(np.float32)
    w["down_wT1"] = np.ascontiguousarray(np.asarray(inputs["L1_ffn_down_weight"]).T).astype(np.float32)
    w["usc1_p"] = np.ascontiguousarray(
        np.asarray(inputs["L1_ffn_up_scale"]).astype(np.float32).reshape(64, 128).T)
    w["dsc1_b"] = np.broadcast_to(
        np.asarray(inputs["L1_ffn_down_scale"]).astype(np.float32), (128, D)).copy()
    w["osc_c"] = np.broadcast_to(
        np.asarray(inputs["L1_ffn_output_scale"]).astype(np.float32).reshape(1, 1),
        (128, 1)).copy()
    return w


FUSED_TENSORS = {
    "x0c": ([TL, D], F32, "ExternalInput"),
    "coff": ([1, 1], I32, "ExternalInput"),
    "ropecos": ([128, SL], F32, "ExternalInput"),
    "ropesin": ([128, SL], F32, "ExternalInput"),
    "kvinitw": ([128, NCORE * H], F32, "ExternalInput"),
    "l1mask": ([128, 1024], F32, "ExternalInput"),
    "ropeRT_bf": ([128, 128], BF16, "ExternalInput"),
    "ident_bf": ([128, 128], BF16, "ExternalInput"),
    "ldiagT_bf": ([H * 128, 128], BF16, "ExternalInput"),
    "qdecb_bf": ([H * 128, 128], BF16, "ExternalInput"),
    "kdecc": ([128, H], F32, "ExternalInput"),
    "blkc": ([128, H], F32, "ExternalInput"),
    "ones1_bf": ([128, 1], BF16, "ExternalInput"),
    "wqT": ([D, D], F32, "ExternalInput"),
    "wkT": ([D, 512], F32, "ExternalInput"),
    "wvT": ([D, 512], F32, "ExternalInput"),
    "woT": ([D, D], F32, "ExternalInput"),
    "gate_wT": ([D, DFF], I32, "ExternalInput"),
    "up_wT": ([D, DFF], I32, "ExternalInput"),
    "down_wT": ([DFF, D], I32, "ExternalInput"),
    "gsc_p": ([128, 64], F32, "ExternalInput"),
    "usc_p": ([128, 64], F32, "ExternalInput"),
    "dsc_b": ([128, D], F32, "ExternalInput"),
    "Wq1": ([D, D], F32, "ExternalInput"),
    "Wk1": ([D, D], F32, "ExternalInput"),
    "WvT": ([D, D], F32, "ExternalInput"),
    "Wo1": ([D, D], F32, "ExternalInput"),
    "up_wT1": ([D, DFF], F32, "ExternalInput"),
    "down_wT1": ([DFF, D], F32, "ExternalInput"),
    "usc1_p": ([128, 64], F32, "ExternalInput"),
    "dsc1_b": ([128, D], F32, "ExternalInput"),
    "osc_c": ([128, 1], F32, "ExternalInput"),
    "h_hbm": ([DFF, TL], BF16, "Internal"),
    "xnT": ([D, TL], BF16, "Internal"),
    "qT": ([D, TL], BF16, "Internal"),
    "kT": ([512, TL], BF16, "Internal"),
    "vtok": ([TL, 512], BF16, "Internal"),
    "o_loc": ([D, TL], BF16, "Internal"),
    "M_core": ([2 * H * 128, 128], F32, "Internal"),
    "M_all": ([NCORE * 2 * H * 128, 128], F32, "Internal", "Shared"),
    "g0": ([TL, D], F32, "Internal"),
    "x1c": ([TL, D], F32, "Internal"),
    "x1all": ([NCORE * TL, D], F32, "Internal", "Shared"),
    "x1s": ([TL, D], F32, "Internal"),
    "Kag": ([2 * H * 128, SL], BF16, "Internal"),
    "Vag": ([TL, D], BF16, "Internal"),
    "Kall": ([NCORE * 2 * H * 128, SL], BF16, "Internal", "Shared"),
    "Vall": ([NCORE * TL, D], BF16, "Internal", "Shared"),
    "h1_hbm": ([DFF, TL], BF16, "Internal"),
    "Q1T": ([D, TL], BF16, "Internal"),
    "g1": ([TL, D], F32, "Internal"),
    "outc": ([TL, D], F32, "ExternalOutput"),
}

FUSED_STAGES = [stage_s0, stage_s1, stage_cc_m, stage_s2, stage_s3, stage_cc_x1,
                stage_s4, stage_cc_kv, stage_extract_x1s, stage_s5, stage_s6]


def kernel(**inputs):
    consts = host_constants()
    pc = host_per_core(inputs, consts)
    w = _prep_weights(inputs)
    maps = []
    for c in range(NCORE):
        m = {
            "x0c": pc[c]["x0c"].astype(np.float32),
            "coff": np.array([[c]], dtype=np.int32),
            "ropecos": pc[c]["ropecos"], "ropesin": pc[c]["ropesin"],
            "kvinitw": pc[c]["kvinitw"], "l1mask": pc[c]["l1mask"],
            "ropeRT_bf": consts["ropeRT_bf"], "ident_bf": consts["ident_bf"],
            "ldiagT_bf": consts["ldiagT_bf"], "qdecb_bf": consts["qdecb_bf"],
            "kdecc": consts["kdecc"], "blkc": consts["blkc"],
            "ones1_bf": consts["ones1_bf"],
        }
        for k2 in ("wqT", "wkT", "wvT", "woT", "gate_wT", "up_wT", "down_wT",
                   "gsc_p", "usc_p", "dsc_b", "Wq1", "Wk1", "WvT", "Wo1",
                   "up_wT1", "down_wT1", "usc1_p", "dsc1_b", "osc_c"):
            m[k2] = w[k2]
        maps.append(m)
    res = run_launch(FUSED_STAGES, FUSED_TENSORS, maps)
    out = np.zeros((2, S, D), dtype=np.float32)
    for c in range(NCORE):
        oc = np.asarray(res[c]["outc"])
        idx = 8 * np.arange(SL) + c
        out[0, idx] = oc[0:SL]
        out[1, idx] = oc[SL:TL]
    return out


def _kernel_unfused(**inputs):

    consts = host_constants()
    pc = host_per_core(inputs, consts)
    w = _prep_weights(inputs)

    t1 = {
        "x0c": ([TL, D], F32, "ExternalInput"),
        "wqT": ([D, D], F32, "ExternalInput"),
        "wkT": ([D, 512], F32, "ExternalInput"),
        "wvT": ([D, 512], F32, "ExternalInput"),
        "ropecos": ([128, SL], F32, "ExternalInput"),
        "ropesin": ([128, SL], F32, "ExternalInput"),
        "ropeRT_bf": ([128, 128], BF16, "ExternalInput"),
        "ident_bf": ([128, 128], BF16, "ExternalInput"),
        "ldiagT_bf": ([H * 128, 128], BF16, "ExternalInput"),
        "qdecb_bf": ([H * 128, 128], BF16, "ExternalInput"),
        "kdecc": ([128, H], F32, "ExternalInput"),
        "blkc": ([128, H], F32, "ExternalInput"),
        "gate_wT": ([D, DFF], I32, "ExternalInput"),
        "up_wT": ([D, DFF], I32, "ExternalInput"),
        "down_wT": ([DFF, D], I32, "ExternalInput"),
        "gsc_p": ([128, 64], F32, "ExternalInput"),
        "usc_p": ([128, 64], F32, "ExternalInput"),
        "dsc_b": ([128, D], F32, "ExternalInput"),
        "ones1_bf": ([128, 1], BF16, "ExternalInput"),
        "h_hbm": ([DFF, TL], BF16, "Internal"),
        "xnT": ([D, TL], BF16, "Internal"),
        "qT": ([D, TL], BF16, "ExternalOutput"),
        "kT": ([512, TL], BF16, "Internal"),
        "vtok": ([TL, 512], BF16, "Internal"),
        "o_loc": ([D, TL], BF16, "ExternalOutput"),
        "M_core": ([2 * H * 128, 128], F32, "ExternalOutput"),
        "g0": ([TL, D], F32, "ExternalOutput"),
    }
    maps1 = []
    for c in range(NCORE):
        maps1.append({
            "x0c": pc[c]["x0c"].astype(np.float32),
            "wqT": w["wqT"], "wkT": w["wkT"], "wvT": w["wvT"],
            "ropecos": pc[c]["ropecos"], "ropesin": pc[c]["ropesin"],
            "ropeRT_bf": consts["ropeRT_bf"], "ident_bf": consts["ident_bf"],
            "ldiagT_bf": consts["ldiagT_bf"], "qdecb_bf": consts["qdecb_bf"],
            "kdecc": consts["kdecc"], "blkc": consts["blkc"],
            "gate_wT": w["gate_wT"], "up_wT": w["up_wT"], "down_wT": w["down_wT"],
            "gsc_p": w["gsc_p"], "usc_p": w["usc_p"], "dsc_b": w["dsc_b"],
            "ones1_bf": consts["ones1_bf"],
        })
    res1 = run_launch([stage_s0, stage_s1, stage_s2], t1, maps1)

    M_all = np.concatenate([np.asarray(r["M_core"]) for r in res1], axis=0)

    t2 = {
        "o_loc": ([D, TL], BF16, "ExternalInput"),
        "qT": ([D, TL], BF16, "ExternalInput"),
        "M_all": ([NCORE * 2 * H * 128, 128], F32, "ExternalInput"),
        "kvinitw": ([128, NCORE * H], F32, "ExternalInput"),
        "qdecb_bf": ([H * 128, 128], BF16, "ExternalInput"),
        "blkc": ([128, H], F32, "ExternalInput"),
        "woT": ([D, D], F32, "ExternalInput"),
        "x0c": ([TL, D], F32, "ExternalInput"),
        "g0": ([TL, D], F32, "ExternalInput"),
        "Wk1": ([D, D], F32, "ExternalInput"),
        "Wv1": ([D, D], F32, "ExternalInput"),
        "ident_bf": ([128, 128], BF16, "ExternalInput"),
        "x1c": ([TL, D], F32, "ExternalOutput"),
        "Kag": ([2 * H * 128, SL], BF16, "ExternalOutput"),
        "Vag": ([TL, D], BF16, "ExternalOutput"),
    }
    maps2 = []
    for c in range(NCORE):
        maps2.append({
            "o_loc": np.asarray(res1[c]["o_loc"]),
            "qT": np.asarray(res1[c]["qT"]),
            "M_all": M_all,
            "kvinitw": pc[c]["kvinitw"],
            "qdecb_bf": consts["qdecb_bf"], "blkc": consts["blkc"],
            "woT": w["woT"],
            "x0c": pc[c]["x0c"].astype(np.float32),
            "g0": np.asarray(res1[c]["g0"]),
            "Wk1": w["Wk1"], "Wv1": w["Wv1"], "ident_bf": consts["ident_bf"],
        })
    res2 = run_launch([stage_s3, stage_s4], t2, maps2)

    Kall = np.concatenate([np.asarray(r["Kag"]) for r in res2], axis=0)
    Vall = np.concatenate([np.asarray(r["Vag"]) for r in res2], axis=0)
    x1g = np.zeros((2, S, D), dtype=np.float32)
    for c in range(NCORE):
        x1c = np.asarray(res2[c]["x1c"])
        for bb in range(2):
            x1g[bb, c * SL:(c + 1) * SL] = x1c[bb * SL:(bb + 1) * SL]
    x1s_list = []
    for c in range(NCORE):
        idx = 8 * np.arange(SL) + c
        x1s_list.append(np.ascontiguousarray(
            np.concatenate([x1g[0, idx], x1g[1, idx]], axis=0)))

    t3 = {
        "x1s": ([TL, D], F32, "ExternalInput"),
        "Wq1": ([D, D], F32, "ExternalInput"),
        "up_wT1": ([D, DFF], F32, "ExternalInput"),
        "down_wT1": ([DFF, D], F32, "ExternalInput"),
        "usc1_p": ([128, 64], F32, "ExternalInput"),
        "dsc1_b": ([128, D], F32, "ExternalInput"),
        "osc_c": ([128, 1], F32, "ExternalInput"),
        "ident_bf": ([128, 128], BF16, "ExternalInput"),
        "Kall": ([NCORE * 2 * H * 128, SL], BF16, "ExternalInput"),
        "Vall": ([NCORE * TL, D], BF16, "ExternalInput"),
        "l1mask": ([128, 1024], F32, "ExternalInput"),
        "Wo1": ([D, D], F32, "ExternalInput"),
        "h1_hbm": ([DFF, TL], BF16, "Internal"),
        "Q1T": ([D, TL], BF16, "Internal"),
        "g1": ([TL, D], F32, "Internal"),
        "outc": ([TL, D], F32, "ExternalOutput"),
    }
    maps3 = []
    for c in range(NCORE):
        maps3.append({
            "x1s": x1s_list[c],
            "Wq1": w["Wq1"], "up_wT1": w["up_wT1"], "down_wT1": w["down_wT1"],
            "usc1_p": w["usc1_p"], "dsc1_b": w["dsc1_b"], "osc_c": w["osc_c"],
            "ident_bf": consts["ident_bf"],
            "Kall": Kall, "Vall": Vall,
            "l1mask": pc[c]["l1mask"], "Wo1": w["Wo1"],
        })
    res3 = run_launch([stage_s5, stage_s6], t3, maps3)

    out = np.zeros((2, S, D), dtype=np.float32)
    for c in range(NCORE):
        oc = np.asarray(res3[c]["outc"])
        idx = 8 * np.arange(SL) + c
        out[0, idx] = oc[0:SL]
        out[1, idx] = oc[SL:TL]
    return out
